# version 02
# baseline (speedup 1.0000x reference)
import numpy as np
import ml_dtypes
from contextlib import ExitStack

import concourse.bass as bass
import concourse.mybir as mybir
from concourse.bass_utils import run_bass_kernel_spmd

BFNP = ml_dtypes.bfloat16
F32 = mybir.dt.float32
BF16 = mybir.dt.bfloat16
AF = mybir.ActivationFunctionType
ALU = mybir.AluOpType
AX = mybir.AxisListType

B, S, D = 2, 4096, 2048
H = 16
ND, RD, VD = 64, 32, 64
QR, KVR = 768, 256
E, RANK, DFF = 8, 8, 5504
NDFF = DFF // 128
NT = 1024
NPT = NT // 128
PFXT = 36
PFX = PFXT * 128
NKC = PFX // 512
M0 = 16.0
SCALE = float(1.0 / np.sqrt(ND + RD))
LNEPS, RMSEPS = 1e-5, 1e-6
NEG = -1e30

A_TILES = list(range(12)) + [32, 33, 34, 35]
B_TILES = list(range(32))


class KB:

    def __init__(self, stop_after=99, taps=(), p4sub=3):
        self.stop_after = stop_after
        self.p4sub = p4sub
        self.taps = set(taps)
        self.tap_names = []
        nc = self.nc = bass.Bass("TRN2", target_bir_lowering=False)
        self.es = ExitStack()
        self.V, self.A, self.T, self.G, self.SY = (
            nc.vector, nc.scalar, nc.tensor, nc.gpsimd, nc.sync)
        self.semh, self.semn = {}, {}
        for s in ("pe", "act", "dve", "st"):
            self._mksem(s)
        self.last_wait = {}
        self._open = []
        self._auto_ld = 0

    def _mksem(self, s):
        self.semh[s] = self.es.enter_context(self.nc.semaphore("sem_" + s))
        self.semn[s] = 0

    def dma(self, eng, out_ap, in_ap, stream=None):
        if stream is None:
            stream = f"ld{self._auto_ld}"
            self._auto_ld += 1
        if stream not in self.semh:
            self._mksem(stream)
        return self.inc(eng.dma_start(out_ap, in_ap), stream, 16)

    def inc(self, inst, sem, amt=1):
        self.semn[sem] += amt
        inst.then_inc(self.semh[sem], amt)
        return (sem, self.semn[sem])

    def wait(self, eng, engname, *evs):
        for ev in evs:
            if ev is None:
                continue
            sem, val = ev
            if val <= 0:
                continue
            key = (engname, sem)
            if self.last_wait.get(key, 0) >= val:
                continue
            eng.wait_ge(self.semh[sem], val)
            self.last_wait[key] = val

    def now(self, sem):
        return (sem, self.semn[sem])

    @staticmethod
    def cpn(sbuf, c, n, off=0):
        import concourse.bass as _b
        return _b.AP(sbuf, off, [[c * n, 128], [n, c], [1, n]])

    def join(self):
        return [self.now(s) for s in list(self.semn)]

    def barrier(self):
        j = self.join()
        self.wait(self.T, "pe", *j)
        self.wait(self.A, "act", *j)
        self.wait(self.V, "dve", *j)
        self.wait(self.SY, "sy", *j)
        self.wait(self.G, "gp", *j)

    def scope(self):
        s = ExitStack()
        self._open.append(s)
        return s

    def sb(self, es, name, shape, dtype, side="left"):
        return es.enter_context(self.nc.sbuf_tensor(name, shape, dtype, side=side))

    def ps(self, es, name, shape, dtype=F32):
        return es.enter_context(self.nc.psum_tensor(name, shape, dtype))

    def close(self, s):
        self._open.remove(s)
        s.close()

    def tap(self, name, ap, shape, dtype):
        if name not in self.taps:
            return
        d = self.nc.dram_tensor("tap_" + name, shape, dtype, kind="ExternalOutput")
        self.barrier()
        self.inc(self.SY.dma_start(d[tuple(slice(0, s) for s in shape)], ap),
                 "st", 16)
        self.tap_names.append("tap_" + name)

    def finish(self):
        for s in reversed(self._open):
            s.close()
        self._open = []
        for s in list(self.semn):
            if (s.startswith("outst") or s.startswith("x2s")
                    or s == "tapst" or s == "st") and self.semn[s] > 0:
                self.SY.wait_ge(self.semh[s], self.semn[s])
        self.es.close()
        return self.nc

    def ln_stats(self, x_ap, n, sqbuf_ap, cols, ci, eps_col, pre_evs, rms=False):
        V, Ae = self.V, self.A
        c = lambda j: cols[:, ci + j:ci + j + 1]
        if not rms:
            self.wait(V, "dve", *pre_evs)
            e_sum = self.inc(V.tensor_reduce(c(2), x_ap, AX.X, ALU.add), "dve")
        self.wait(Ae, "act", *pre_evs)
        e_ssq = self.inc(Ae.activation(sqbuf_ap, x_ap, AF.Square,
                                       accum_out=c(3)), "act")
        if rms:
            self.wait(V, "dve", e_ssq)
            e = self.inc(V.tensor_scalar_mul(c(4), c(3), 1.0 / n), "dve")
            self.wait(Ae, "act", e)
            e = self.inc(Ae.activation(c(5), c(4), AF.Ln, bias=eps_col), "act")
            self.wait(Ae, "act", e)
            return self.inc(Ae.activation(c(0), c(5), AF.Exp, scale=-0.5), "act")
        self.wait(V, "dve", e_sum, e_ssq)
        e = self.inc(V.tensor_scalar_mul(c(2), c(2), 1.0 / n), "dve")
        self.wait(V, "dve", e)
        e = self.inc(V.tensor_scalar_mul(c(3), c(3), 1.0 / n), "dve")
        self.wait(V, "dve", e)
        e = self.inc(V.tensor_scalar(c(4), c(2), c(2), None, ALU.mult), "dve")
        self.wait(V, "dve", e)
        e = self.inc(V.tensor_sub(c(4), c(3), c(4)), "dve")
        self.wait(Ae, "act", e)
        e = self.inc(Ae.activation(c(5), c(4), AF.Ln, bias=eps_col), "act")
        self.wait(Ae, "act", e)
        e_r = self.inc(Ae.activation(c(0), c(5), AF.Exp, scale=-0.5), "act")
        self.wait(V, "dve", e_r)
        return self.inc(V.tensor_scalar(c(1), c(2), c(0), -1.0,
                                        ALU.mult, ALU.mult), "dve")

    def build(self):
        nc, es = self.nc, self.es
        V, Ae, T, G, SY = self.V, self.A, self.T, self.G, self.SY

        def din(name, shape, dtype=F32):
            return nc.dram_tensor(name, shape, dtype, kind="ExternalInput")

        d_own_tm = din("own_tm", [NT, D])
        d_own_fm = din("own_fm", [D, NT])
        d_xp_tm = din("xp_tm", [PFX, D])
        d_xp_fm = din("xp_fm", [D, PFX])
        d_cosq = din("cosq", [128, NT])
        d_cosk = din("cosk", [PFX, 16])
        d_biasa = din("biasa", [128, 16])
        d_biasb = din("biasb", [128, 32])
        d_rbm = din("rbm", [4, 128, 512], BF16)
        d_ident = din("ident", [128, 128], BF16)
        d_wcq = din("w_cq", [D, QR])
        d_wqcat = din("wq_cat", [QR, H * 128])
        d_wkv = din("wkv_cat", [D, 288])
        d_wkn = din("w_kn", [KVR, H * ND])
        d_wv = din("w_v", [KVR, H * VD])
        d_wproj = din("w_proj", [H * VD, D])
        d_wra = din("w_ra", [D, 152])
        d_bgc = din("bgc", [72, DFF])
        d_buc = din("buc", [72, DFF])
        d_adc = din("adc", [DFF, 72])
        d_bdc = din("bdc", [72, D])
        d_wg = din("wg", [D, DFF])
        d_wu = din("wu", [D, DFF])
        d_wd = din("wd", [DFF, D])
        d_gcol = din("gcol", [128, 6])
        d_kvcol = din("kvcol", [128, 2])
        d_eb = din("eb", [1, 8])

        d_out = nc.dram_tensor("out", [NT, D], F32, kind="ExternalOutput")
        d_x2 = nc.dram_tensor("x2scr", [NT, D], F32, kind="ExternalOutput")

        s_ident = self.sb(es, "sb_ident", [128, 128], BF16)
        s_ones1 = self.sb(es, "ones1", [1, 128], F32)
        s_onesq = self.sb(es, "onesq", [128, 128], BF16)
        s_eps = self.sb(es, "eps", [128, 2], F32)
        s_csbcq = self.sb(es, "csbcq", [128, QR], F32)
        s_csbkv = self.sb(es, "csbkv", [128, 288], F32)
        s_ebb = self.sb(es, "ebb", [128, 8], F32)
        s_ebr = self.sb(es, "sb_ebr", [1, 8], F32)
        s_c1 = self.sb(es, "c1", [128, 2 * NPT + 6], F32)
        s_c2 = self.sb(es, "c2", [128, 2 * NPT + 6], F32)
        s_wx = self.sb(es, "sb_wx", [128, 8 * NPT], F32)
        s_cp = self.sb(es, "cp", [128, 8], F32)
        s_cg = self.sb(es, "cg", [128, 8], F32)

        e_ident = self.dma(SY, s_ident[:, :], d_ident[:, :])
        e_ebr = self.dma(SY, s_ebr[:, :], d_eb[:, :])
        e_ones1 = self.inc(V.memset(s_ones1[:, :], 1.0), "dve")
        e_onesq = self.inc(V.memset(s_onesq[:, :], 1.0), "dve")
        e_eps = self.inc(V.memset(s_eps[:, 0:1], LNEPS), "dve")
        self.wait(V, "dve", e_eps)
        self.inc(V.memset(s_eps[:, 1:2], RMSEPS), "dve")
        eps5, eps6 = s_eps[:, 0:1], s_eps[:, 1:2]

        esAO = self.scope()
        s_nkv = self.sb(esAO, "nkv", [128, 2 * PFX], BF16)
        s_kasm = self.sb(esAO, "kasm", [128, PFX], BF16)
        s_nq = self.sb(esAO, "nq", [128, 6 * NT], BF16)
        s_vgrp = self.sb(esAO, "vgrp", [128, PFXT * 260], BF16)
        s_qh = self.sb(esAO, "qh", [128, NT], BF16)
        s_pb = [self.sb(esAO, f"pb{i}", [128, 512], BF16) for i in range(2)]
        s_rbm = self.sb(esAO, "sb_rbm", [128, 4 * 512], BF16)
        s_cosq = self.sb(esAO, "sb_cosq", [128, NT], F32)
        s_cosk = self.sb(esAO, "sb_cosk", [128, PFXT * 16], F32)
        s_biasa = self.sb(esAO, "sb_biasa", [128, 16], F32)
        s_biasb = self.sb(esAO, "sb_biasb", [128, 32], F32)
        s_wq = self.sb(esAO, "sb_wq", [128, 6 * H * 128], BF16)
        s_wkn = self.sb(esAO, "sb_wkn", [128, 2 * H * ND], BF16)
        s_wv = self.sb(esAO, "sb_wv", [128, 2 * H * VD], BF16)
        s_row = self.sb(esAO, "rowb", [1, 512], F32)
        s_rb = self.sb(esAO, "rbuf", [64, 512], F32)
        s_gkcol = self.sb(esAO, "gkcol", [128, 8], F32)

        esA1 = self.scope()
        s_wcq = self.sb(esA1, "sb_wcq", [128, 16 * QR], BF16)
        s_wkv = self.sb(esA1, "sb_wkv", [128, 16 * 288], BF16)
        s_xt = [self.sb(esA1, f"xt{i}", [128, D], F32) for i in range(2)]
        s_xf = [self.sb(esA1, f"xf{i}", [128, D], BF16) for i in range(2)]
        s_sq = self.sb(esA1, "sqscr", [128, D], BF16)
        s_ckv = [self.sb(esA1, f"ckv{i}", [128, 288], F32) for i in range(2)]
        s_kro = self.sb(esA1, "kro", [128, 32], BF16)
        s_cq = self.sb(esA1, "cqf", [128, QR], F32)
        s_t768 = self.sb(esA1, "t768", [128, QR], F32)
        s_nqtm = self.sb(esA1, "nqtm", [128, QR], BF16)
        s_scr16 = self.sb(esA1, "scr16", [128, 256], BF16)

        psA1 = self.scope()
        p_ckv = [self.ps(psA1, f"p_ckv{i}", [128, 288]) for i in range(2)]
        p_trn = self.ps(psA1, "p_trn", [128, 256], BF16)
        p_trk = self.ps(psA1, "p_trk", [32, 128], BF16)
        p_cq = [self.ps(psA1, f"p_cq{i}", [128, 384]) for i in range(2)]
        p_trq = self.ps(psA1, "p_trq", [128, QR], BF16)

        e_rbm = self.dma(SY,
            bass.AP(s_rbm, 0, [[4 * 512, 128], [512, 4], [1, 512]]),
            d_rbm[:, :, :].rearrange("t p q -> p t q"))
        e_ba = self.dma(SY, s_biasa[:, :], d_biasa[:, :])
        e_bb = self.dma(SY, s_biasb[:, :], d_biasb[:, :])
        e_cosq = self.dma(SY, s_cosq[:, :], d_cosq[:, :])
        e_cosk = self.dma(SY,
            self.cpn(s_cosk, PFXT, 16),
            d_cosk[:, :].rearrange("(t p) f -> p t f", p=128))
        e_gcol = self.dma(SY, s_gkcol[:, 0:6], d_gcol[:, :])
        e_kvcol = self.dma(SY, s_gkcol[:, 6:8], d_kvcol[:, :])
        rr = lambda d: d[:, :].rearrange("(c p) n -> p c n", p=128)
        e_wcq = self.dma(G, self.cpn(s_wcq, 16, QR), rr(d_wcq))
        e_wq = self.dma(G, self.cpn(s_wq, 6, 2048), rr(d_wqcat))
        e_wkv = self.dma(G, self.cpn(s_wkv, 16, 288), rr(d_wkv))
        e_wkn = self.dma(G, self.cpn(s_wkn, 2, 1024), rr(d_wkn))
        e_wv = self.dma(G, self.cpn(s_wv, 2, 1024), rr(d_wv))

        e_wqp = None
        for c in range(6):
            self.wait(V, "dve", e_wq, e_gcol)
            e = self.inc(V.tensor_scalar_mul(
                s_wq[:, c * 2048:(c + 1) * 2048],
                s_wq[:, c * 2048:(c + 1) * 2048], s_gkcol[:, c:c + 1]), "dve")
            ev = bass.AP(s_wq, c * 2048 + 64, [[6 * 2048, 128], [128, 16], [1, 16]])
            od = bass.AP(s_wq, c * 2048 + 80, [[6 * 2048, 128], [128, 16], [1, 16]])
            scr = bass.AP(s_scr16, 0, [[256, 128], [16, 16], [1, 16]])
            self.wait(V, "dve", e, e_wqp)
            e = self.inc(V.tensor_copy(scr, ev), "dve")
            self.wait(V, "dve", e)
            e = self.inc(V.tensor_sub(ev, scr, od), "dve")
            self.wait(V, "dve", e)
            e_wqp = self.inc(V.tensor_add(od, scr, od), "dve")
        e_wkvp = e_wqp
        for c in range(16):
            ev = bass.AP(s_wkv, c * 288 + 256, [[16 * 288, 128], [1, 16]])
            od = bass.AP(s_wkv, c * 288 + 272, [[16 * 288, 128], [1, 16]])
            scr = bass.AP(s_scr16, 0, [[256, 128], [1, 16]])
            self.wait(V, "dve", e_wkv, e_wkvp)
            e = self.inc(V.tensor_copy(scr, ev), "dve")
            self.wait(V, "dve", e)
            e = self.inc(V.tensor_sub(ev, scr, od), "dve")
            self.wait(V, "dve", e)
            e_wkvp = self.inc(V.tensor_add(od, scr, od), "dve")
        e_wknp = e_wvp = None
        for c in range(2):
            self.wait(V, "dve", e_wkn, e_wv, e_kvcol)
            e_wknp = self.inc(V.tensor_scalar_mul(
                s_wkn[:, c * 1024:(c + 1) * 1024],
                s_wkn[:, c * 1024:(c + 1) * 1024], s_gkcol[:, 6 + c:7 + c]), "dve")
            e_wvp = self.inc(V.tensor_scalar_mul(
                s_wv[:, c * 1024:(c + 1) * 1024],
                s_wv[:, c * 1024:(c + 1) * 1024], s_gkcol[:, 6 + c:7 + c]), "dve")
        e_csb1 = None
        self.wait(T, "pe", e_wcq, self.now("dve"))
        for half in range(2):
            for c in range(16):
                ii = T.matmul(
                    p_cq[half][:, :], s_onesq[:, :],
                    s_wcq[:, c * QR + half * 384:c * QR + half * 384 + 384],
                    start=(c == 0), stop=(c == 15))
            e_mm = self.inc(ii, "pe")
            self.wait(Ae, "act", e_mm)
            e_csb1 = self.inc(Ae.copy(s_csbcq[:, half * 384:half * 384 + 384],
                                      p_cq[half][:, :]), "act")
        self.wait(T, "pe", e_wkvp)
        for c in range(16):
            ii = T.matmul(p_ckv[0][:, :], s_onesq[:, :],
                          s_wkv[:, c * 288:(c + 1) * 288],
                          start=(c == 0), stop=(c == 15))
        e_mm = self.inc(ii, "pe")
        self.wait(Ae, "act", e_mm)
        e_csb2 = self.inc(Ae.copy(s_csbkv[:, :], p_ckv[0][:, :]), "act")
        self.wait(T, "pe", e_ebr, self.now("act"))
        e_mm = self.inc(T.matmul(p_cq[0][:, 0:8], s_ones1[0:1, 0:128],
                                 s_ebr[:, :], start=True, stop=True), "pe")
        self.wait(Ae, "act", e_mm)
        e_ebb = self.inc(Ae.copy(s_ebb[:, :], p_cq[0][:, 0:8]), "act")
        e_kz = self.inc(V.memset(s_kasm[64:128, :], 0.0), "dve")

        ev_xt, ev_xf = [None, None], [None, None]
        last_rd_xt, last_rd_xf = [None, None], [None, None]
        last_rd_ckv = [None, None]
        e_ckv_wr = [None, None]
        prev_evac = None
        prev_rope = None
        e_nkv_ev = e_kro_ev = None
        for t in range(PFXT):
            bi = t % 2
            self.wait(SY, "sy", last_rd_xt[bi])
            ev_xt[bi] = self.dma(
                SY, s_xt[bi][:, :], d_xp_tm[128 * t:128 * t + 128, :],
                f"xt{bi}")
            self.wait(G, "gp", last_rd_xf[bi])
            ev_xf[bi] = self.dma(
                G, self.cpn(s_xf[bi], 16, 128),
                d_xp_fm[:, 128 * t:128 * t + 128].rearrange(
                    "(c p) k -> p c k", p=128), f"xf{bi}")
            e_st = self.ln_stats(s_xt[bi][:, :], D, s_sq[:, :], s_cp, 0, eps5,
                                 [ev_xt[bi]])
            last_rd_xt[bi] = e_st
            self.wait(T, "pe", ev_xf[bi], e_wkvp if t == 0 else None,
                      e_ckv_wr[bi])
            for c in range(16):
                ii = T.matmul(p_ckv[bi][:, :], s_xf[bi][:, c * 128:(c + 1) * 128],
                              s_wkv[:, c * 288:(c + 1) * 288],
                              start=(c == 0), stop=(c == 15))
            e_mm = self.inc(ii, "pe")
            last_rd_xf[bi] = e_mm
            self.wait(V, "dve", e_st, e_mm, e_csb2 if t == 0 else None,
                      last_rd_ckv[bi])
            e = self.inc(V.tensor_scalar_mul(s_ckv[bi][:, :], s_csbkv[:, :],
                                             s_cp[:, 1:2]), "dve")
            self.wait(V, "dve", e)
            e_corr = self.inc(V.scalar_tensor_tensor(
                s_ckv[bi][:, :], p_ckv[bi][:, :], s_cp[:, 0:1], s_ckv[bi][:, :],
                ALU.mult, ALU.add), "dve")
            e_ckv_wr[bi] = e_corr
            self.wait(Ae, "act", e_corr)
            e_ssq = self.inc(Ae.activation(s_sq[:, 0:256], s_ckv[bi][:, 0:256],
                                           AF.Square, accum_out=s_cp[:, 7:8]),
                             "act")
            self.wait(V, "dve", e_ssq)
            e = self.inc(V.tensor_scalar_mul(s_cp[:, 7:8], s_cp[:, 7:8],
                                             1.0 / KVR), "dve")
            self.wait(Ae, "act", e)
            e = self.inc(Ae.activation(s_cp[:, 6:7], s_cp[:, 7:8], AF.Ln,
                                       bias=eps6), "act")
            self.wait(Ae, "act", e)
            e_rr = self.inc(Ae.activation(s_cp[:, 6:7], s_cp[:, 6:7], AF.Exp,
                                          scale=-0.5), "act")
            self.wait(Ae, "act", e_rr, e_nkv_ev)
            e_nkvap = self.inc(Ae.activation(s_nqtm[:, 0:256], s_ckv[bi][:, 0:256],
                                             AF.Identity, bias=0.0,
                                             scale=s_cp[:, 6:7]), "act")
            e_nkv_ev = e_nkvap
            cosr = bass.AP(s_cosk, 16 * t, [[PFXT * 16, 128], [0, 2], [1, 16]])
            self.wait(V, "dve", e_corr, e_cosk, e_kro_ev)
            e_kro = self.inc(V.tensor_tensor(s_kro[:, :], s_ckv[bi][:, 256:288],
                                             cosr, ALU.mult), "dve")
            e_kro_ev = e_kro
            last_rd_ckv[bi] = e_kro
            self.wait(T, "pe", e_nkvap, e_kro, e_ident if t == 0 else None,
                      prev_evac)
            T.transpose(p_trn[:, 0:128], s_nqtm[:, 0:128], s_ident[:, :])
            T.transpose(p_trn[:, 128:256], s_nqtm[:, 128:256], s_ident[:, :])
            ii = T.transpose(p_trk[0:32, :], s_kro[:, :], s_ident[:, :])
            e_tr = self.inc(ii, "pe")
            self.wait(Ae, "act", e_tr)
            Ae.copy(s_nkv[:, 128 * t:128 * t + 128], p_trn[:, 0:128])
            Ae.copy(s_nkv[:, PFX + 128 * t:PFX + 128 * t + 128],
                    p_trn[:, 128:256])
            self.wait(Ae, "act", e_kz if t == 0 else None)
            prev_evac = self.inc(Ae.copy(s_kasm[64:96, 128 * t:128 * t + 128],
                                         p_trk[0:32, :]), "act")
            e_nkv_ev = prev_evac
            e_kro_ev = prev_evac
        self.tap("nkv", s_nkv[:, :], [128, 2 * PFX], BF16)
        self.tap("kasm", s_kasm[64:128, :], [64, PFX], BF16)
        if self.stop_after < 2:
            return self.finish()

        e_nq_ev = None
        e_cq_wr = None
        for tt in range(NPT):
            bi = tt % 2
            self.wait(SY, "sy", last_rd_xt[bi])
            ev_xt[bi] = self.dma(
                SY, s_xt[bi][:, :], d_own_tm[128 * tt:128 * tt + 128, :],
                f"xt{bi}")
            self.wait(G, "gp", last_rd_xf[bi])
            ev_xf[bi] = self.dma(
                G, self.cpn(s_xf[bi], 16, 128),
                d_own_fm[:, 128 * tt:128 * tt + 128].rearrange(
                    "(c p) k -> p c k", p=128), f"xf{bi}")
            e_st = self.ln_stats(s_xt[bi][:, :], D, s_sq[:, :], s_c1, 2 * NPT,
                                 eps5, [ev_xt[bi]])
            last_rd_xt[bi] = e_st
            self.wait(V, "dve", e_st)
            e = self.inc(V.tensor_copy(s_c1[:, tt:tt + 1],
                                       s_c1[:, 2 * NPT:2 * NPT + 1]), "dve")
            self.wait(V, "dve", e)
            e_cp = self.inc(V.tensor_copy(s_c1[:, NPT + tt:NPT + tt + 1],
                                          s_c1[:, 2 * NPT + 1:2 * NPT + 2]),
                            "dve")
            self.wait(T, "pe", ev_xf[bi], e_wcq if tt == 0 else None, e_cq_wr)
            for half in range(2):
                for c in range(16):
                    ii = T.matmul(
                        p_cq[half][:, :], s_xf[bi][:, c * 128:(c + 1) * 128],
                        s_wcq[:, c * QR + half * 384:c * QR + half * 384 + 384],
                        start=(c == 0), stop=(c == 15))
            e_mm = self.inc(ii, "pe")
            last_rd_xf[bi] = e_mm
            self.wait(V, "dve", e_mm, e_cp, e_csb1 if tt == 0 else None)
            e = self.inc(V.tensor_scalar_mul(
                s_t768[:, :], s_csbcq[:, :], s_c1[:, NPT + tt:NPT + tt + 1]),
                "dve")
            self.wait(V, "dve", e)
            e = self.inc(V.scalar_tensor_tensor(
                s_cq[:, 0:384], p_cq[0][:, :], s_c1[:, tt:tt + 1],
                s_t768[:, 0:384], ALU.mult, ALU.add), "dve")
            self.wait(V, "dve", e)
            e_corr = self.inc(V.scalar_tensor_tensor(
                s_cq[:, 384:768], p_cq[1][:, :], s_c1[:, tt:tt + 1],
                s_t768[:, 384:768], ALU.mult, ALU.add), "dve")
            e_cq_wr = e_corr
            e_rr = self.ln_stats(s_cq[:, :], QR, s_sq[:, 0:QR], s_cg, 0, eps6,
                                 [e_corr], rms=True)
            self.wait(Ae, "act", e_rr, e_nq_ev)
            e_ap = self.inc(Ae.activation(s_nqtm[:, :], s_cq[:, :], AF.Identity,
                                          bias=0.0, scale=s_cg[:, 0:1]), "act")
            self.wait(T, "pe", e_ap)
            for c in range(6):
                ii = T.transpose(p_trq[:, c * 128:(c + 1) * 128],
                                 s_nqtm[:, c * 128:(c + 1) * 128], s_ident[:, :])
            e_tr = self.inc(ii, "pe")
            dst = bass.AP(s_nq, 128 * tt, [[6 * NT, 128], [NT, 6], [1, 128]])
            self.wait(Ae, "act", e_tr)
            e_nq_ev = self.inc(Ae.copy(dst, p_trq[:, :]), "act")
        self.tap("nq", s_nq[:, :], [128, 6 * NT], BF16)
        self.close(psA1)
        self.close(esA1)
        if self.stop_after < 3:
            return self.finish()

        esM2 = self.scope()
        s_ln2 = self.sb(esM2, "ln2fm", [128, 16 * NT], BF16, side="right")
        esM1 = self.scope()
        s_ao = self.sb(esM1, "attnout", [128, 8 * NT], BF16, side="right")
        s_wproj = self.sb(esM1, "sb_wproj", [128, 8 * D], BF16, side="right")

        psAT = self.scope()
        p_s = [self.ps(psAT, f"p_s{i}", [128, 512]) for i in range(2)]
        p_av = [self.ps(psAT, f"p_av{i}", [128, 512]) for i in range(2)]
        p_bc = self.ps(psAT, "p_bc", [64, 512])

        self.barrier()
        e_wproj = self.dma(G, self.cpn(s_wproj, 8, D), rr(d_wproj))

        ev_pe_last_score = None
        ev_norm = [None, None]
        e_vev = [None, None]
        e_kev = [None, None]
        e_kev_last = None
        e_q_done = None
        prev_bcm = prev_rbv = prev_nm = None
        for hg in range(4):
            ones_pat = bass.AP(s_vgrp, 64, [[260 * PFXT, 128], [260, PFXT],
                                            [65, 4], [1, 1]])
            self.wait(V, "dve", self.now("pe"))
            e_ones = self.inc(V.memset(ones_pat, 1.0), "dve")
            for t in range(PFXT):
                bi = t % 2
                self.wait(T, "pe", e_wvp if hg == 0 and t == 0 else None,
                          ev_norm[bi] if t < 2 else None,
                          e_vev[bi])
                for rc in range(2):
                    ii = T.matmul(
                        p_av[bi][:, 0:256],
                        s_nkv[:, rc * PFX + 128 * t:rc * PFX + 128 * t + 128],
                        s_wv[:, rc * 1024 + 256 * hg:rc * 1024 + 256 * hg + 256],
                        start=(rc == 0), stop=(rc == 1))
                e_mm = self.inc(ii, "pe")
                dst = bass.AP(s_vgrp, 260 * t, [[260 * PFXT, 128], [65, 4],
                                                [1, 64]])
                self.wait(V, "dve", e_mm, e_ones)
                e_vev[bi] = self.inc(V.tensor_copy(dst, p_av[bi][:, 0:256]),
                                     "dve")
            for hj in range(4):
                h = 4 * hg + hj
                for kc in range(NKC):
                    bi = kc % 2
                    self.wait(T, "pe",
                              e_wknp if h == 0 and kc == 0 else None,
                              e_kev[bi])
                    for rc in range(2):
                        ii = T.matmul(
                            p_s[bi][0:64, :],
                            s_wkn[:, rc * 1024 + ND * h:rc * 1024 + ND * h + ND],
                            s_nkv[:, rc * PFX + 512 * kc:
                                  rc * PFX + 512 * kc + 512],
                            start=(rc == 0), stop=(rc == 1))
                    e_mm = self.inc(ii, "pe")
                    self.wait(Ae, "act", e_mm, ev_pe_last_score)
                    e_kev[bi] = self.inc(
                        Ae.copy(s_kasm[0:64, 512 * kc:512 * kc + 512],
                                p_s[bi][0:64, :]), "act")
                e_kev_last = e_kev[(NKC - 1) % 2]
                e_cm_q = [None, None]
                for qc in range(2):
                    self.wait(T, "pe", e_wqp if h == 0 and qc == 0 else None,
                              ev_norm[qc], e_vev[qc])
                    for r in range(6):
                        ii = T.matmul(
                            p_av[qc][:, :],
                            s_wq[:, r * 2048 + 128 * h:r * 2048 + 128 * h + 128],
                            s_nq[:, r * NT + 512 * qc:r * NT + 512 * qc + 512],
                            start=(r == 0), stop=(r == 5))
                    e_mm = self.inc(ii, "pe")
                    self.wait(V, "dve", e_mm,
                              e_cosq if h == 0 and qc == 0 else None)
                    e_cm = self.inc(V.tensor_tensor(
                        p_av[qc][64:96, :], p_av[qc][64:96, :],
                        s_cosq[64:96, 512 * qc:512 * qc + 512], ALU.mult), "dve")
                    e_cm_q[qc] = e_cm
                    self.wait(Ae, "act", e_cm)
                    e_q_done = self.inc(
                        Ae.copy(s_qh[:, 512 * qc:512 * qc + 512],
                                p_av[qc][:, :]), "act")
                for ph, (tiles, qc, bias_s, diag0) in enumerate(
                        [(A_TILES, 0, s_biasa, 12), (B_TILES, 1, s_biasb, 28)]):
                    n_it = len(tiles)
                    q0 = 512 * qc
                    ev_exp = [None] * n_it
                    ev_sc = [None] * n_it

                    def col0_of(i):
                        return 128 * (i - diag0) if i >= diag0 else 0

                    def emit_score(i):
                        t = tiles[i]
                        c0 = col0_of(i)
                        masked = i >= diag0
                        self.wait(T, "pe",
                                  e_kev_last if i == 0 else None,
                                  e_q_done if i == 0 else None,
                                  ev_exp[i - 2] if i >= 2 else None)
                        ii = T.matmul(p_s[i % 2][:, c0:512],
                                      s_kasm[:, 128 * t:128 * t + 128],
                                      s_qh[:, q0 + c0:q0 + 512],
                                      start=True, stop=not masked)
                        if masked:
                            ri = i - diag0
                            self.wait(T, "pe",
                                      e_rbm if h == 0 and ph == 0 else None)
                            ii = T.matmul(p_s[i % 2][:, c0:512], s_ident[:, :],
                                          s_rbm[:, 512 * ri + c0:512 * ri + 512],
                                          start=False, stop=True)
                        ev_sc[i] = self.inc(ii, "pe")

                    def emit_exp(i):
                        c0 = col0_of(i)
                        self.wait(Ae, "act", ev_sc[i],
                                  (e_ba if ph == 0 else e_bb) if h == 0 else None)
                        ev_exp[i] = self.inc(Ae.activation(
                            s_pb[i % 2][:, c0:512], p_s[i % 2][:, c0:512],
                            AF.Exp, bias=bias_s[:, i:i + 1], scale=SCALE), "act")

                    def emit_av(i):
                        t = tiles[i]
                        c0 = col0_of(i)
                        self.wait(T, "pe", ev_exp[i],
                                  ev_norm[qc] if i == 0 else None,
                                  e_cm_q[qc] if i == 0 else None,
                                  e_vev[0] if i == 0 else None,
                                  e_vev[1] if i == 0 else None)
                        ii = T.matmul(p_av[qc][0:65, c0:512],
                                      s_vgrp[:, 260 * t + 65 * hj:
                                             260 * t + 65 * hj + 65],
                                      s_pb[i % 2][:, c0:512],
                                      start=(i == 0), stop=(i == n_it - 1))
                        if i == n_it - 1:
                            return self.inc(ii, "pe")
                        return None

                    emit_score(0)
                    emit_exp(0)
                    emit_score(1)
                    emit_exp(1)
                    e_av_done = None
                    for i in range(n_it):
                        r = emit_av(i)
                        if r is not None:
                            e_av_done = r
                        if i + 2 < n_it:
                            emit_score(i + 2)
                            emit_exp(i + 2)
                    ev_pe_last_score = ev_sc[n_it - 1]
                    self.wait(V, "dve", e_av_done, prev_bcm)
                    e_rec = self.inc(V.reciprocal(s_row[:, :],
                                                  p_av[qc][64:65, :]), "dve")
                    self.wait(T, "pe", e_rec, prev_rbv)
                    e_bcm = self.inc(T.matmul(p_bc[:, :], s_ones1[0:1, 0:64],
                                              s_row[:, :], start=True,
                                              stop=True), "pe")
                    prev_bcm = e_bcm
                    self.wait(Ae, "act", e_bcm, prev_nm)
                    e_rbv = self.inc(Ae.copy(s_rb[:, :], p_bc[:, :]), "act")
                    prev_rbv = e_rbv
                    ao_off = (h // 2) * NT + q0
                    ao_p0 = 64 * (h % 2)
                    self.wait(V, "dve", e_rbv)
                    e_nm = self.inc(V.tensor_mul(
                        s_ao[ao_p0:ao_p0 + 64, ao_off:ao_off + 512],
                        p_av[qc][0:64, :], s_rb[:, :]), "dve")
                    prev_nm = e_nm
                    ev_norm[qc] = e_nm
        self.tap("attnout", s_ao[:, :], [128, 8 * NT], BF16)
        self.close(psAT)
        self.close(esAO)
        if self.stop_after < 4:
            return self.finish()

        psM = self.scope()
        p_proj = self.ps(psM, "p_proj", [128, 2048])
        p_ltr = self.ps(psM, "p_ltr", [128, 2048], BF16)
        esP4 = self.scope()
        s_xo = [self.sb(esP4, f"xo{i}", [128, D], F32) for i in range(2)]
        s_lnap = [self.sb(esP4, f"lnap{i}", [128, D], F32) for i in range(2)]
        s_ln2tm = self.sb(esP4, "ln2tm", [128, D], BF16)
        s_sq4 = self.sb(esP4, "sq4", [128, D], BF16)

        self.barrier()

        ev_xo = [None, None]
        last_rd_xo = [[], []]
        last_rd_lnap = [None, None]
        e_ltr_ev = None
        e_proj_rd = None
        e_x2st = [None] * NPT
        for tt in range(NPT):
            bi = tt % 2
            self.wait(SY, "sy", *last_rd_xo[bi])
            ev_xo[bi] = self.dma(
                SY, s_xo[bi][:, :], d_own_tm[128 * tt:128 * tt + 128, :],
                f"xo{bi}")
            self.wait(T, "pe", e_wproj if tt == 0 else None, e_proj_rd)
            for cs in range(4):
                for c in range(8):
                    ii = T.matmul(
                        p_proj[:, 512 * cs:512 * cs + 512],
                        s_ao[:, c * NT + 128 * tt:c * NT + 128 * tt + 128],
                        s_wproj[:, c * D + 512 * cs:c * D + 512 * cs + 512],
                        start=(c == 0), stop=(c == 7))
            e_mm = self.inc(ii, "pe")
            if self.p4sub < 1:
                self.wait(Ae, "act", e_mm, last_rd_lnap[bi])
                e_ap0 = self.inc(Ae.copy(s_lnap[bi][:, :], p_proj[:, :]), "act")
                e_proj_rd = e_ap0
                last_rd_lnap[bi] = e_ap0
                last_rd_xo[bi] = [ev_xo[bi]]
                continue
            self.wait(Ae, "act", e_mm, last_rd_lnap[bi])
            e_cp0 = self.inc(Ae.copy(s_lnap[bi][:, :], p_proj[:, :]), "act")
            e_proj_rd = e_cp0
            e_st = self.ln_stats(s_lnap[bi][:, :], D, s_sq4[:, :], s_cg, 0,
                                 eps5, [e_cp0])
            self.wait(Ae, "act", e_st)
            e_ap = self.inc(Ae.activation(s_lnap[bi][:, :], s_lnap[bi][:, :],
                                          AF.Identity, bias=s_cg[:, 1:2],
                                          scale=s_cg[:, 0:1]), "act")
            self.wait(V, "dve", e_ap, ev_xo[bi])
            e_x2 = self.inc(V.tensor_add(s_xo[bi][:, :], s_xo[bi][:, :],
                                         s_lnap[bi][:, :]), "dve")
            last_rd_lnap[bi] = e_x2
            self.wait(SY, "sy", e_x2)
            e_x2st[tt] = self.dma(
                SY, d_x2[128 * tt:128 * tt + 128, :], s_xo[bi][:, :],
                f"x2s{bi}")
            if self.p4sub < 2:
                last_rd_xo[bi] = [e_x2, e_x2st[tt]]
                last_rd_lnap[bi] = e_x2
                continue
            e_st2 = self.ln_stats(s_xo[bi][:, :], D, s_sq4[:, :], s_c2, 2 * NPT,
                                  eps5, [e_x2])
            self.wait(V, "dve", e_st2)
            e = self.inc(V.tensor_copy(s_c2[:, tt:tt + 1],
                                       s_c2[:, 2 * NPT:2 * NPT + 1]), "dve")
            self.wait(V, "dve", e)
            e_cp = self.inc(V.tensor_copy(s_c2[:, NPT + tt:NPT + tt + 1],
                                          s_c2[:, 2 * NPT + 1:2 * NPT + 2]),
                            "dve")
            self.wait(Ae, "act", e_cp, e_ltr_ev)
            e_ap2 = self.inc(Ae.activation(s_ln2tm[:, :], s_xo[bi][:, :],
                                           AF.Identity,
                                           bias=s_c2[:, NPT + tt:NPT + tt + 1],
                                           scale=s_c2[:, tt:tt + 1]), "act")
            last_rd_xo[bi] = [e_ap2, e_x2st[tt]]
            if self.p4sub < 3:
                continue
            self.wait(T, "pe", e_ap2, e_ltr_ev)
            for c in range(16):
                ii = T.transpose(p_ltr[:, c * 128:(c + 1) * 128],
                                 s_ln2tm[:, c * 128:(c + 1) * 128],
                                 s_ident[:, :])
            e_tr = self.inc(ii, "pe")
            self.wait(Ae, "act", e_tr)
            for hh in range(2):
                dst = bass.AP(s_ln2, (8 * hh) * NT + 128 * tt,
                              [[16 * NT, 128], [NT, 8], [1, 128]])
                e_ltr_ev = self.inc(
                    Ae.copy(dst, p_ltr[:, hh * 1024:hh * 1024 + 1024]), "act")
        self.tap("ln2fm", s_ln2[:, :], [128, 16 * NT], BF16)
        self.close(psM)
        self.close(esP4)
        self.close(esM1)
        if self.stop_after < 5:
            return self.finish()

        esF = self.scope()
        s_hid = self.sb(esF, "hidden", [128, NDFF * 512], BF16)
        s_facc = self.sb(esF, "facc", [128, 4 * D], F32)
        s_wra = self.sb(esF, "sb_wra", [128, 16 * 152], BF16)
        s_adc = self.sb(esF, "sb_adc", [128, NDFF * 72], BF16)
        s_bgc = self.sb(esF, "sb_bgc", [72, DFF], BF16)
        s_buc = self.sb(esF, "sb_buc", [72, DFF], BF16)
        s_bdc = self.sb(esF, "sb_bdc", [72, D], BF16)
        s_wgb = [self.sb(esF, f"wgb{i}", [128, 2048], BF16) for i in range(2)]
        s_wub = [self.sb(esF, f"wub{i}", [128, 2048], BF16) for i in range(2)]
        s_wdb = [self.sb(esF, f"wdb{i}", [128, 512], BF16) for i in range(2)]
        s_xaw = self.sb(esF, "xaw", [128, 144], BF16)
        s_haw = self.sb(esF, "haw", [128, 72], BF16)
        s_xawg = self.sb(esF, "xawg", [72, 512], BF16)
        s_xawu = self.sb(esF, "xawu", [72, 512], BF16)
        s_hawd = self.sb(esF, "hawd", [72, 512], BF16)
        s_sg = self.sb(esF, "sgate", [128, 512], BF16)
        s_rc = self.sb(esF, "rcols", [128, 32], F32)
        s_xo2 = [self.sb(esF, f"xo2{i}", [128, D], F32) for i in range(2)]
        s_fap = [self.sb(esF, f"fap{i}", [128, D], F32) for i in range(2)]
        s_sq5 = self.sb(esF, "sq5", [128, D], BF16)

        psF = self.scope()
        p_g = self.ps(psF, "p_g", [128, 512])
        p_u = self.ps(psF, "p_u", [128, 512])
        p_xa = self.ps(psF, "p_xa", [128, 152])
        p_t72 = self.ps(psF, "p_t72", [72, 256], BF16)
        p_d = [self.ps(psF, f"p_d{i}", [128, 512]) for i in range(4)]

        self.barrier()

        e_wra = self.dma(G, self.cpn(s_wra, 16, 152), rr(d_wra))
        e_adc = self.dma(G, self.cpn(s_adc, NDFF, 72), rr(d_adc))
        e_bgc = self.dma(G, s_bgc[:, :], d_bgc[:, :])
        e_buc = self.dma(G, s_buc[:, :], d_buc[:, :])
        e_bdc = self.dma(G, s_bdc[:, :], d_bdc[:, :])

        ev_wg = [None, None]
        rd_wg = [None, None]
        ev_wu = [None, None]
        rd_wu = [None, None]
        ev_wd = [None, None]
        rd_wd = [None, None]
        e_si = None
        e_mul = None
        e_xa_rd = None
        e_xa_rd_act = None
        e_t72_rd = None
        e_down_mm_last = None
        e_ep_dve_last = None
        last_rd_xo2 = [[], []]
        last_wr_fap = [None, None]
        for half in range(2):
            q0 = 512 * half
            e_xaw_tr = []
            for tt in range(4):
                gt = 4 * half + tt
                self.wait(T, "pe", e_wra if gt == 0 else None, e_xa_rd,
                          e_xa_rd_act)
                for c in range(16):
                    ii = T.matmul(
                        p_xa[:, :],
                        s_ln2[:, c * NT + 128 * gt:c * NT + 128 * gt + 128],
                        s_wra[:, c * 152:(c + 1) * 152],
                        start=(c == 0), stop=(c == 15))
                e_mm = self.inc(ii, "pe")
                r = s_rc
                self.wait(Ae, "act", e_mm)
                e_sig = self.inc(Ae.activation(r[:, 0:8], p_xa[:, 0:8],
                                               AF.Sigmoid), "act")
                e_xa_rd_act = e_sig
                self.wait(V, "dve", e_sig, e_ebb if gt == 0 else None)
                e = self.inc(V.tensor_add(r[:, 8:16], r[:, 0:8], s_ebb[:, :]),
                             "dve")
                self.wait(V, "dve", e)
                e = self.inc(V.tensor_reduce(r[:, 24:25], r[:, 8:16], AX.X,
                                             ALU.max), "dve")
                self.wait(V, "dve", e)
                e = self.inc(V.tensor_scalar(r[:, 16:24], r[:, 8:16],
                                             r[:, 24:25], None, ALU.is_equal),
                             "dve")
                self.wait(V, "dve", e)
                e = self.inc(V.scalar_tensor_tensor(r[:, 16:24], r[:, 16:24],
                                                    NEG, r[:, 8:16], ALU.mult,
                                                    ALU.add), "dve")
                self.wait(V, "dve", e)
                e = self.inc(V.tensor_reduce(r[:, 25:26], r[:, 16:24], AX.X,
                                             ALU.max), "dve")
                self.wait(V, "dve", e)
                e = self.inc(V.tensor_scalar(r[:, 16:24], r[:, 8:16],
                                             r[:, 25:26], None, ALU.is_ge),
                             "dve")
                self.wait(V, "dve", e)
                e = self.inc(V.tensor_mul(r[:, 16:24], r[:, 16:24], r[:, 0:8]),
                             "dve")
                self.wait(V, "dve", e)
                e = self.inc(V.tensor_reduce(r[:, 26:27], r[:, 16:24], AX.X,
                                             ALU.add), "dve")
                self.wait(V, "dve", e)
                e = self.inc(V.tensor_scalar_add(r[:, 26:27], r[:, 26:27],
                                                 1e-8), "dve")
                self.wait(V, "dve", e)
                e = self.inc(V.reciprocal(r[:, 27:28], r[:, 26:27]), "dve")
                self.wait(V, "dve", e)
                e_wx = self.inc(V.tensor_scalar(
                    s_wx[:, 8 * gt:8 * gt + 8], r[:, 16:24], r[:, 27:28],
                    0.125, ALU.mult, ALU.mult), "dve")
                wrep = bass.AP(s_wx, 8 * gt, [[8 * NPT, 128], [1, 8], [0, 8]])
                self.wait(V, "dve", e_wx)
                self.inc(V.tensor_scalar_mul(s_xaw[:, 0:8], p_xa[:, 8:16],
                                             0.125), "dve")
                self.inc(V.tensor_tensor(s_xaw[:, 8:72], p_xa[:, 16:80], wrep,
                                         ALU.mult), "dve")
                self.inc(V.tensor_scalar_mul(s_xaw[:, 72:80], p_xa[:, 80:88],
                                             0.125), "dve")
                e_xa_rd = self.inc(V.tensor_tensor(s_xaw[:, 80:144],
                                                   p_xa[:, 88:152], wrep,
                                                   ALU.mult), "dve")
                self.wait(T, "pe", e_xa_rd, e_t72_rd)
                T.transpose(p_t72[0:72, 0:128], s_xaw[:, 0:72], s_ident[:, :])
                ii = T.transpose(p_t72[0:72, 128:256], s_xaw[:, 72:144],
                                 s_ident[:, :])
                e_tr = self.inc(ii, "pe")
                self.wait(Ae, "act", e_tr, e_down_mm_last)
                Ae.copy(s_xawg[0:72, 128 * tt:128 * tt + 128],
                        p_t72[0:72, 0:128])
                e_t72_rd = self.inc(
                    Ae.copy(s_xawu[0:72, 128 * tt:128 * tt + 128],
                            p_t72[0:72, 128:256]), "act")
                e_xaw_tr.append(e_t72_rd)
            for j in range(NDFF):
                bi = j % 2
                self.wait(G, "gp", rd_wg[bi])
                ev_wg[bi] = self.dma(
                    G, self.cpn(s_wgb[bi], 16, 128),
                    d_wg[:, 128 * j:128 * j + 128].rearrange(
                        "(c p) n -> p c n", p=128), f"wg{bi}")
                self.wait(G, "gp", rd_wu[bi])
                ev_wu[bi] = self.dma(
                    G, self.cpn(s_wub[bi], 16, 128),
                    d_wu[:, 128 * j:128 * j + 128].rearrange(
                        "(c p) n -> p c n", p=128), f"wu{bi}")
                self.wait(T, "pe", ev_wg[bi], e_xaw_tr[-1] if j == 0 else None,
                          e_bgc if half == 0 and j == 0 else None, e_si)
                for c in range(16):
                    T.matmul(p_g[:, :], s_wgb[bi][:, c * 128:(c + 1) * 128],
                             s_ln2[:, c * NT + q0:c * NT + q0 + 512],
                             start=(c == 0), stop=False)
                e_gmm = self.inc(T.matmul(p_g[:, :], s_bgc[0:72, 128 * j:
                                                           128 * j + 128],
                                          s_xawg[0:72, :], start=False,
                                          stop=True), "pe")
                rd_wg[bi] = e_gmm
                self.wait(T, "pe", ev_wu[bi],
                          e_buc if half == 0 and j == 0 else None, e_mul)
                for c in range(16):
                    T.matmul(p_u[:, :], s_wub[bi][:, c * 128:(c + 1) * 128],
                             s_ln2[:, c * NT + q0:c * NT + q0 + 512],
                             start=(c == 0), stop=False)
                e_umm = self.inc(T.matmul(p_u[:, :], s_buc[0:72, 128 * j:
                                                           128 * j + 128],
                                          s_xawu[0:72, :], start=False,
                                          stop=True), "pe")
                rd_wu[bi] = e_umm
                self.wait(Ae, "act", e_gmm, e_mul)
                e_si = self.inc(Ae.activation(s_sg[:, :], p_g[:, :], AF.Silu),
                                "act")
                self.wait(V, "dve", e_si, e_umm,
                          e_down_mm_last if j == 0 else None)
                e_mul = self.inc(V.tensor_mul(
                    s_hid[:, 512 * j:512 * j + 512], s_sg[:, :], p_u[:, :]),
                    "dve")
            e_hid_done = e_mul
            e_haw_tr = []
            for tt in range(4):
                gt = 4 * half + tt
                self.wait(T, "pe", e_hid_done if tt == 0 else None,
                          e_adc if half == 0 and tt == 0 else None,
                          e_xa_rd, e_xa_rd_act)
                for j in range(NDFF):
                    ii = T.matmul(
                        p_xa[:, 0:72],
                        s_hid[:, 512 * j + 128 * tt:512 * j + 128 * tt + 128],
                        s_adc[:, 72 * j:72 * j + 72],
                        start=(j == 0), stop=(j == NDFF - 1))
                e_mm = self.inc(ii, "pe")
                wrep = bass.AP(s_wx, 8 * gt, [[8 * NPT, 128], [1, 8], [0, 8]])
                self.wait(V, "dve", e_mm)
                self.inc(V.tensor_scalar_mul(s_haw[:, 0:8], p_xa[:, 0:8],
                                             0.125), "dve")
                e_xa_rd = self.inc(V.tensor_tensor(s_haw[:, 8:72],
                                                   p_xa[:, 8:72], wrep,
                                                   ALU.mult), "dve")
                self.wait(T, "pe", e_xa_rd, e_t72_rd)
                ii = T.transpose(p_t72[0:72, 0:128], s_haw[:, 0:72],
                                 s_ident[:, :])
                e_tr = self.inc(ii, "pe")
                self.wait(Ae, "act", e_tr, e_down_mm_last)
                e_t72_rd = self.inc(
                    Ae.copy(s_hawd[0:72, 128 * tt:128 * tt + 128],
                            p_t72[0:72, 0:128]), "act")
                e_haw_tr.append(e_t72_rd)
            e_dev = [None] * 4
            for cq in range(4):
                for j in range(NDFF):
                    bi = j % 2
                    self.wait(G, "gp", rd_wd[bi])
                    ev_wd[bi] = self.dma(
                        G, s_wdb[bi][:, :],
                        d_wd[128 * j:128 * j + 128, 512 * cq:512 * cq + 512],
                        f"wd{bi}")
                    if j == 0:
                        self.wait(T, "pe", *e_dev)
                        self.wait(T, "pe", e_ep_dve_last)
                    self.wait(T, "pe", ev_wd[bi])
                    for tt in range(4):
                        ii = T.matmul(
                            p_d[tt][:, :],
                            s_hid[:, 512 * j + 128 * tt:
                                  512 * j + 128 * tt + 128],
                            s_wdb[bi][:, :],
                            start=(j == 0), stop=False)
                    rd_wd[bi] = self.inc(ii, "pe")
                self.wait(T, "pe", e_haw_tr[-1] if cq == 0 else None,
                          e_bdc if half == 0 and cq == 0 else None)
                for tt in range(4):
                    e_bmm = self.inc(T.matmul(
                        p_d[tt][:, :], s_hawd[0:72, 128 * tt:128 * tt + 128],
                        s_bdc[0:72, 512 * cq:512 * cq + 512],
                        start=False, stop=True), "pe")
                    e_down_mm_last = e_bmm
                    self.wait(Ae, "act", e_bmm,
                              e_ep_dve_last if cq == 0 else None)
                    e_dev[tt] = self.inc(
                        Ae.copy(s_facc[:, 2048 * tt + 512 * cq:
                                       2048 * tt + 512 * cq + 512],
                                p_d[tt][:, :]), "act")
            for tt in range(4):
                gt = 4 * half + tt
                bi = tt % 2
                e_st = self.ln_stats(s_facc[:, 2048 * tt:2048 * tt + 2048], D,
                                     s_sq5[:, :], s_cg, 0, eps5,
                                     [e_dev[tt]])
                e_ep_dve_last = e_st
                self.wait(SY, "sy", e_x2st[gt], *last_rd_xo2[bi])
                ev_x2 = self.dma(
                    SY, s_xo2[bi][:, :], d_x2[128 * gt:128 * gt + 128, :],
                    f"xo2_{bi}")
                self.wait(Ae, "act", e_st, last_wr_fap[bi])
                e_ap = self.inc(Ae.activation(
                    s_fap[bi][:, :], s_facc[:, 2048 * tt:2048 * tt + 2048],
                    AF.Identity, bias=s_cg[:, 1:2], scale=s_cg[:, 0:1]), "act")
                self.wait(V, "dve", e_ap, ev_x2)
                e_fin = self.inc(V.tensor_add(s_xo2[bi][:, :], s_xo2[bi][:, :],
                                              s_fap[bi][:, :]), "dve")
                last_wr_fap[bi] = e_fin
                self.wait(SY, "sy", e_fin)
                e_ost = self.dma(
                    SY, d_out[128 * gt:128 * gt + 128, :], s_xo2[bi][:, :],
                    f"outst{bi}")
                last_rd_xo2[bi] = [e_fin, e_ost]
        return self.finish()


_FREQS = (1.0 / 10000.0 ** (np.arange(0, RD, 2, dtype=np.float64) / RD))


def _cos_table(pos):
    ang = np.outer(pos.astype(np.float64), _FREQS)
    return np.cos(ang).astype(np.float32)


def _pack_shared(inp):
    f32 = lambda k: np.asarray(inp[k], dtype=np.float32)
    w_q_nope = f32("L0_attn_w_q_nope")
    w_q_rope = f32("L0_attn_w_q_rope")
    wq_cat = np.zeros((QR, H * 128), np.float32)
    for h in range(H):
        wq_cat[:, 128 * h:128 * h + 64] = w_q_nope[:, 64 * h:64 * h + 64]
        wq_cat[:, 128 * h + 64:128 * h + 80] = w_q_rope[:, 32 * h:32 * h + 32:2]
        wq_cat[:, 128 * h + 80:128 * h + 96] = w_q_rope[:, 32 * h + 1:
                                                        32 * h + 32:2]
    w_k_rope = f32("L0_attn_w_k_rope")
    wkv_cat = np.concatenate(
        [f32("L0_attn_w_ckv"), w_k_rope[:, 0::2], w_k_rope[:, 1::2]], axis=1)
    A_g = np.ascontiguousarray(f32("L0_ffn_A_gate").transpose(1, 0, 2)
                               ).reshape(D, 64)
    A_u = np.ascontiguousarray(f32("L0_ffn_A_up").transpose(1, 0, 2)
                               ).reshape(D, 64)
    w_ra = np.concatenate([f32("L0_ffn_W_router"), f32("L0_ffn_sA_gate"), A_g,
                           f32("L0_ffn_sA_up"), A_u], axis=1)
    bgc = np.concatenate([f32("L0_ffn_sB_gate"),
                          f32("L0_ffn_B_gate").reshape(64, DFF)], axis=0)
    buc = np.concatenate([f32("L0_ffn_sB_up"),
                          f32("L0_ffn_B_up").reshape(64, DFF)], axis=0)
    A_d = np.ascontiguousarray(f32("L0_ffn_A_down").transpose(1, 0, 2)
                               ).reshape(DFF, 64)
    adc = np.concatenate([f32("L0_ffn_sA_down"), A_d], axis=1)
    bdc = np.concatenate([f32("L0_ffn_sB_down"),
                          f32("L0_ffn_B_down").reshape(64, D)], axis=0)
    shared = {
        "ident": np.eye(128, dtype=BFNP),
        "rbm": _make_rbm(),
        "w_cq": f32("L0_attn_w_cq"),
        "wq_cat": wq_cat,
        "wkv_cat": np.ascontiguousarray(wkv_cat),
        "w_kn": f32("L0_attn_w_k_nope"),
        "w_v": f32("L0_attn_w_v"),
        "w_proj": f32("L0_attn_w_proj"),
        "w_ra": np.ascontiguousarray(w_ra),
        "bgc": np.ascontiguousarray(bgc),
        "buc": np.ascontiguousarray(buc),
        "adc": np.ascontiguousarray(adc),
        "bdc": np.ascontiguousarray(bdc),
        "wg": f32("L0_ffn_W_gate"),
        "wu": f32("L0_ffn_W_up"),
        "wd": f32("L0_ffn_W_down"),
        "gcol": np.ascontiguousarray(
            f32("L0_attn_q_g").reshape(6, 128).T),
        "kvcol": np.ascontiguousarray(
            f32("L0_attn_kv_g").reshape(2, 128).T),
        "eb": f32("L0_ffn_expert_bias").reshape(1, E),
    }
    return shared


def _make_rbm():
    rbm = np.zeros((4, 128, 512), np.float32)
    kk = np.arange(128)
    for i in range(4):
        for j in range(4):
            blk = rbm[i][:, 128 * j:128 * j + 128]
            if j < i:
                blk[:] = NEG
            elif j == i:
                blk[:] = np.where(kk[:, None] > kk[None, :], NEG, 0.0)
    return rbm.astype(BFNP)


def _pack_core(c, x_full):
    p = c % 4
    b = c // 4
    x = np.asarray(x_full[b], dtype=np.float32)
    blk = lambda i: x[512 * i:512 * (i + 1)]
    own = np.concatenate([blk(p), blk(7 - p)], axis=0)
    pos_own = np.concatenate([np.arange(512 * p, 512 * (p + 1)),
                              np.arange(512 * (7 - p), 512 * (8 - p))])
    storage = np.concatenate(
        [x[0:512 * (7 - p)], np.zeros((512 * p, D), np.float32),
         blk(7 - p), blk(p)], axis=0)
    pos_st = np.concatenate(
        [np.arange(0, 512 * (7 - p)), np.zeros(512 * p, np.int64),
         np.arange(512 * (7 - p), 512 * (8 - p)),
         np.arange(512 * p, 512 * (p + 1))])
    cosq = np.zeros((128, NT), np.float32)
    ct = _cos_table(pos_own).T
    cosq[64:80, :] = ct
    cosq[80:96, :] = ct
    cosk = _cos_table(pos_st) / H
    biasa = np.full((128, 16), -M0, np.float32)
    for it in range(12):
        if it >= 4 * p:
            biasa[:, it] = NEG
    biasb = np.full((128, 32), -M0, np.float32)
    for t in range(32):
        if 4 * (7 - p) <= t < 28:
            biasb[:, t] = NEG
    return {
        "own_tm": np.ascontiguousarray(own),
        "own_fm": np.ascontiguousarray(own.T),
        "xp_tm": np.ascontiguousarray(storage),
        "xp_fm": np.ascontiguousarray(storage.T),
        "cosq": cosq,
        "cosk": np.ascontiguousarray(cosk),
        "biasa": biasa,
        "biasb": biasb,
    }


_CACHED = {"nc": None}


def kernel(**inputs):
    if _CACHED["nc"] is None:
        _CACHED["nc"] = KB().build()
    nc = _CACHED["nc"]
    shared = _pack_shared(inputs)
    x_full = np.asarray(inputs["L0_attn_x"], dtype=np.float32)
    in_maps = []
    for c in range(8):
        m = dict(shared)
        m.update(_pack_core(c, x_full))
        in_maps.append(m)
    res = run_bass_kernel_spmd(nc, in_maps, core_ids=list(range(8)))
    out = np.empty((B, S, D), np.float32)
    for c in range(8):
        p, b = c % 4, c // 4
        o = res.results[c]["out"]
        out[b, 512 * p:512 * (p + 1)] = o[0:512]
        out[b, 512 * (7 - p):512 * (8 - p)] = o[512:1024]
    return out


# version 03 (final)
# speedup vs baseline: 1.2108x; 1.2108x over previous
import numpy as np
import ml_dtypes
from contextlib import ExitStack

import concourse.bass as bass
import concourse.mybir as mybir
from concourse.bass_utils import run_bass_kernel_spmd

BFNP = ml_dtypes.bfloat16
F32 = mybir.dt.float32
BF16 = mybir.dt.bfloat16
AF = mybir.ActivationFunctionType
ALU = mybir.AluOpType
AX = mybir.AxisListType

B, S, D = 2, 4096, 2048
H = 16
ND, RD, VD = 64, 32, 64
QR, KVR = 768, 256
E, RANK, DFF = 8, 8, 5504
NDFF = DFF // 128
NT = 1024
NPT = NT // 128
PFXT = 36
PFX = PFXT * 128
NKC = PFX // 512
M0 = 16.0
SCALE = float(1.0 / np.sqrt(ND + RD))
LNEPS, RMSEPS = 1e-5, 1e-6
NEG = -1e30

A_TILES = list(range(12)) + [32, 33, 34, 35]
B_TILES = list(range(32))


class KB:

    def __init__(self, stop_after=99, taps=(), p4sub=3):
        self.stop_after = stop_after
        self.p4sub = p4sub
        self.taps = set(taps)
        self.tap_names = []
        nc = self.nc = bass.Bass("TRN2", target_bir_lowering=False)
        self.es = ExitStack()
        self.V, self.A, self.T, self.G, self.SY = (
            nc.vector, nc.scalar, nc.tensor, nc.gpsimd, nc.sync)
        self.semh, self.semn = {}, {}
        for s in ("pe", "act", "dve", "st"):
            self._mksem(s)
        self.last_wait = {}
        self._open = []
        self._auto_ld = 0

    def _mksem(self, s):
        self.semh[s] = self.es.enter_context(self.nc.semaphore("sem_" + s))
        self.semn[s] = 0

    def dma(self, eng, out_ap, in_ap, stream=None):
        if stream is None:
            stream = f"ld{self._auto_ld}"
            self._auto_ld += 1
        if stream not in self.semh:
            self._mksem(stream)
        return self.inc(eng.dma_start(out_ap, in_ap), stream, 16)

    def inc(self, inst, sem, amt=1):
        self.semn[sem] += amt
        inst.then_inc(self.semh[sem], amt)
        return (sem, self.semn[sem])

    def wait(self, eng, engname, *evs):
        for ev in evs:
            if ev is None:
                continue
            sem, val = ev
            if val <= 0:
                continue
            key = (engname, sem)
            if self.last_wait.get(key, 0) >= val:
                continue
            eng.wait_ge(self.semh[sem], val)
            self.last_wait[key] = val

    def now(self, sem):
        return (sem, self.semn[sem])

    @staticmethod
    def cpn(sbuf, c, n, off=0):
        import concourse.bass as _b
        return _b.AP(sbuf, off, [[c * n, 128], [n, c], [1, n]])

    def join(self):
        return [self.now(s) for s in list(self.semn)]

    def barrier(self):
        j = self.join()
        self.wait(self.T, "pe", *j)
        self.wait(self.A, "act", *j)
        self.wait(self.V, "dve", *j)
        self.wait(self.SY, "sy", *j)
        self.wait(self.G, "gp", *j)

    def scope(self):
        s = ExitStack()
        self._open.append(s)
        return s

    def sb(self, es, name, shape, dtype, side="left"):
        return es.enter_context(self.nc.sbuf_tensor(name, shape, dtype, side=side))

    def ps(self, es, name, shape, dtype=F32):
        return es.enter_context(self.nc.psum_tensor(name, shape, dtype))

    def close(self, s):
        self._open.remove(s)
        s.close()

    def tap(self, name, ap, shape, dtype):
        if name not in self.taps:
            return
        d = self.nc.dram_tensor("tap_" + name, shape, dtype, kind="ExternalOutput")
        self.barrier()
        self.inc(self.SY.dma_start(d[tuple(slice(0, s) for s in shape)], ap),
                 "st", 16)
        self.tap_names.append("tap_" + name)

    def finish(self):
        for s in reversed(self._open):
            s.close()
        self._open = []
        for s in list(self.semn):
            if (s.startswith("outst") or s.startswith("x2s")
                    or s == "tapst" or s == "st") and self.semn[s] > 0:
                self.SY.wait_ge(self.semh[s], self.semn[s])
        self.es.close()
        return self.nc

    def ln_stats(self, x_ap, n, sqbuf_ap, cols, ci, eps_col, pre_evs, rms=False):
        V, Ae = self.V, self.A
        c = lambda j: cols[:, ci + j:ci + j + 1]
        if not rms:
            self.wait(V, "dve", *pre_evs)
            e_sum = self.inc(V.tensor_reduce(c(2), x_ap, AX.X, ALU.add), "dve")
        self.wait(Ae, "act", *pre_evs)
        e_ssq = self.inc(Ae.activation(sqbuf_ap, x_ap, AF.Square,
                                       accum_out=c(3)), "act")
        if rms:
            self.wait(V, "dve", e_ssq)
            e = self.inc(V.tensor_scalar_mul(c(4), c(3), 1.0 / n), "dve")
            self.wait(Ae, "act", e)
            e = self.inc(Ae.activation(c(5), c(4), AF.Ln, bias=eps_col), "act")
            self.wait(Ae, "act", e)
            return self.inc(Ae.activation(c(0), c(5), AF.Exp, scale=-0.5), "act")
        self.wait(V, "dve", e_sum, e_ssq)
        e = self.inc(V.tensor_scalar_mul(c(2), c(2), 1.0 / n), "dve")
        self.wait(V, "dve", e)
        e = self.inc(V.tensor_scalar_mul(c(3), c(3), 1.0 / n), "dve")
        self.wait(V, "dve", e)
        e = self.inc(V.tensor_scalar(c(4), c(2), c(2), None, ALU.mult), "dve")
        self.wait(V, "dve", e)
        e = self.inc(V.tensor_sub(c(4), c(3), c(4)), "dve")
        self.wait(Ae, "act", e)
        e = self.inc(Ae.activation(c(5), c(4), AF.Ln, bias=eps_col), "act")
        self.wait(Ae, "act", e)
        e_r = self.inc(Ae.activation(c(0), c(5), AF.Exp, scale=-0.5), "act")
        self.wait(V, "dve", e_r)
        return self.inc(V.tensor_scalar(c(1), c(2), c(0), -1.0,
                                        ALU.mult, ALU.mult), "dve")

    def build(self):
        nc, es = self.nc, self.es
        V, Ae, T, G, SY = self.V, self.A, self.T, self.G, self.SY

        def din(name, shape, dtype=F32):
            return nc.dram_tensor(name, shape, dtype, kind="ExternalInput")

        d_own_tm = din("own_tm", [NT, D])
        d_own_fm = din("own_fm", [D, NT])
        d_xp_tm = din("xp_tm", [PFX, D])
        d_xp_fm = din("xp_fm", [D, PFX])
        d_cosq = din("cosq", [128, NT])
        d_cosk = din("cosk", [PFX, 16])
        d_biasa = din("biasa", [128, 16])
        d_biasb = din("biasb", [128, 32])
        d_rbm = din("rbm", [4, 128, 512], BF16)
        d_ident = din("ident", [128, 128], BF16)
        d_wcq = din("w_cq", [D, QR])
        d_wqcat = din("wq_cat", [QR, H * 128])
        d_wkv = din("wkv_cat", [D, 288])
        d_wkn = din("w_kn", [KVR, H * ND])
        d_wv = din("w_v", [KVR, H * VD])
        d_wproj = din("w_proj", [H * VD, D])
        d_wra = din("w_ra", [D, 152])
        d_bgc = din("bgc", [72, DFF])
        d_buc = din("buc", [72, DFF])
        d_adc = din("adc", [DFF, 72])
        d_bdc = din("bdc", [72, D])
        d_wg = din("wg", [D, DFF])
        d_wu = din("wu", [D, DFF])
        d_wd = din("wd", [DFF, D])
        d_gcol = din("gcol", [128, 6])
        d_kvcol = din("kvcol", [128, 2])
        d_eb = din("eb", [1, 8])

        d_out = nc.dram_tensor("out", [NT, D], F32, kind="ExternalOutput")
        d_x2 = nc.dram_tensor("x2scr", [NT, D], F32, kind="ExternalOutput")

        s_ident = self.sb(es, "sb_ident", [128, 128], BF16)
        s_ones1 = self.sb(es, "ones1", [1, 128], F32)
        s_onesq = self.sb(es, "onesq", [128, 128], BF16)
        s_eps = self.sb(es, "eps", [128, 2], F32)
        s_csbcq = self.sb(es, "csbcq", [128, QR], F32)
        s_csbkv = self.sb(es, "csbkv", [128, 288], F32)
        s_ebb = self.sb(es, "ebb", [128, 8], F32)
        s_ebr = self.sb(es, "sb_ebr", [1, 8], F32)
        s_c1 = self.sb(es, "c1", [128, 2 * NPT + 6], F32)
        s_c2 = self.sb(es, "c2", [128, 2 * NPT + 6], F32)
        s_wx = self.sb(es, "sb_wx", [128, 8 * NPT], F32)
        s_cp = self.sb(es, "cp", [128, 8], F32)
        s_cg = self.sb(es, "cg", [128, 8], F32)

        e_ident = self.dma(SY, s_ident[:, :], d_ident[:, :])
        e_ebr = self.dma(SY, s_ebr[:, :], d_eb[:, :])
        e_ones1 = self.inc(V.memset(s_ones1[:, :], 1.0), "dve")
        e_onesq = self.inc(V.memset(s_onesq[:, :], 1.0), "dve")
        e_eps = self.inc(V.memset(s_eps[:, 0:1], LNEPS), "dve")
        self.wait(V, "dve", e_eps)
        self.inc(V.memset(s_eps[:, 1:2], RMSEPS), "dve")
        eps5, eps6 = s_eps[:, 0:1], s_eps[:, 1:2]

        esAO = self.scope()
        s_nkv = self.sb(esAO, "nkv", [128, 2 * PFX], BF16)
        s_kasm = self.sb(esAO, "kasm", [128, PFX], BF16)
        s_nq = self.sb(esAO, "nq", [128, 6 * NT], BF16)
        s_vgrp = self.sb(esAO, "vgrp", [128, PFXT * 260], BF16)
        s_qh = self.sb(esAO, "qh", [128, NT], BF16)
        s_pb = [self.sb(esAO, f"pb{i}", [128, 512], BF16) for i in range(2)]
        s_rbm = self.sb(esAO, "sb_rbm", [128, 4 * 512], BF16)
        s_cosq = self.sb(esAO, "sb_cosq", [128, NT], F32)
        s_cosk = self.sb(esAO, "sb_cosk", [128, PFXT * 16], F32)
        s_biasa = self.sb(esAO, "sb_biasa", [128, 16], F32)
        s_biasb = self.sb(esAO, "sb_biasb", [128, 32], F32)
        s_wq = self.sb(esAO, "sb_wq", [128, 6 * H * 128], BF16)
        s_wkn = self.sb(esAO, "sb_wkn", [128, 2 * H * ND], BF16)
        s_wv = self.sb(esAO, "sb_wv", [128, 2 * H * VD], BF16)
        s_row = self.sb(esAO, "rowb", [1, 512], F32)
        s_rb = self.sb(esAO, "rbuf", [64, 512], F32)
        s_gkcol = self.sb(esAO, "gkcol", [128, 8], F32)

        esA1 = self.scope()
        s_wcq = self.sb(esA1, "sb_wcq", [128, 16 * QR], BF16)
        s_wkv = self.sb(esA1, "sb_wkv", [128, 16 * 288], BF16)
        s_xt = [self.sb(esA1, f"xt{i}", [128, D], F32) for i in range(2)]
        s_xf = [self.sb(esA1, f"xf{i}", [128, D], BF16) for i in range(2)]
        s_sq = self.sb(esA1, "sqscr", [128, D], BF16)
        s_ckv = [self.sb(esA1, f"ckv{i}", [128, 288], F32) for i in range(2)]
        s_kro = self.sb(esA1, "kro", [128, 32], BF16)
        s_cq = self.sb(esA1, "cqf", [128, QR], F32)
        s_t768 = self.sb(esA1, "t768", [128, QR], F32)
        s_nqtm = self.sb(esA1, "nqtm", [128, QR], BF16)
        s_scr16 = self.sb(esA1, "scr16", [128, 256], BF16)

        psA1 = self.scope()
        p_ckv = [self.ps(psA1, f"p_ckv{i}", [128, 288]) for i in range(2)]
        p_trn = self.ps(psA1, "p_trn", [128, 256], BF16)
        p_trk = self.ps(psA1, "p_trk", [32, 128], BF16)
        p_cq = [self.ps(psA1, f"p_cq{i}", [128, 384]) for i in range(2)]
        p_trq = self.ps(psA1, "p_trq", [128, QR], BF16)

        e_rbm = self.dma(SY,
            bass.AP(s_rbm, 0, [[4 * 512, 128], [512, 4], [1, 512]]),
            d_rbm[:, :, :].rearrange("t p q -> p t q"))
        e_ba = self.dma(SY, s_biasa[:, :], d_biasa[:, :])
        e_bb = self.dma(SY, s_biasb[:, :], d_biasb[:, :])
        e_cosq = self.dma(SY, s_cosq[:, :], d_cosq[:, :])
        e_cosk = self.dma(SY,
            self.cpn(s_cosk, PFXT, 16),
            d_cosk[:, :].rearrange("(t p) f -> p t f", p=128))
        e_gcol = self.dma(SY, s_gkcol[:, 0:6], d_gcol[:, :])
        e_kvcol = self.dma(SY, s_gkcol[:, 6:8], d_kvcol[:, :])
        rr = lambda d: d[:, :].rearrange("(c p) n -> p c n", p=128)
        e_wcq = self.dma(G, self.cpn(s_wcq, 16, QR), rr(d_wcq))
        e_wq = self.dma(G, self.cpn(s_wq, 6, 2048), rr(d_wqcat))
        e_wkv = self.dma(G, self.cpn(s_wkv, 16, 288), rr(d_wkv))
        e_wkn = self.dma(G, self.cpn(s_wkn, 2, 1024), rr(d_wkn))
        e_wv = self.dma(G, self.cpn(s_wv, 2, 1024), rr(d_wv))

        e_wqp = None
        for c in range(6):
            self.wait(V, "dve", e_wq, e_gcol)
            e = self.inc(V.tensor_scalar_mul(
                s_wq[:, c * 2048:(c + 1) * 2048],
                s_wq[:, c * 2048:(c + 1) * 2048], s_gkcol[:, c:c + 1]), "dve")
            ev = bass.AP(s_wq, c * 2048 + 64, [[6 * 2048, 128], [128, 16], [1, 16]])
            od = bass.AP(s_wq, c * 2048 + 80, [[6 * 2048, 128], [128, 16], [1, 16]])
            scr = bass.AP(s_scr16, 0, [[256, 128], [16, 16], [1, 16]])
            self.wait(V, "dve", e, e_wqp)
            e = self.inc(V.tensor_copy(scr, ev), "dve")
            self.wait(V, "dve", e)
            e = self.inc(V.tensor_sub(ev, scr, od), "dve")
            self.wait(V, "dve", e)
            e_wqp = self.inc(V.tensor_add(od, scr, od), "dve")
        e_wkvp = e_wqp
        for c in range(16):
            ev = bass.AP(s_wkv, c * 288 + 256, [[16 * 288, 128], [1, 16]])
            od = bass.AP(s_wkv, c * 288 + 272, [[16 * 288, 128], [1, 16]])
            scr = bass.AP(s_scr16, 0, [[256, 128], [1, 16]])
            self.wait(V, "dve", e_wkv, e_wkvp)
            e = self.inc(V.tensor_copy(scr, ev), "dve")
            self.wait(V, "dve", e)
            e = self.inc(V.tensor_sub(ev, scr, od), "dve")
            self.wait(V, "dve", e)
            e_wkvp = self.inc(V.tensor_add(od, scr, od), "dve")
        e_wknp = e_wvp = None
        for c in range(2):
            self.wait(V, "dve", e_wkn, e_wv, e_kvcol)
            e_wknp = self.inc(V.tensor_scalar_mul(
                s_wkn[:, c * 1024:(c + 1) * 1024],
                s_wkn[:, c * 1024:(c + 1) * 1024], s_gkcol[:, 6 + c:7 + c]), "dve")
            e_wvp = self.inc(V.tensor_scalar_mul(
                s_wv[:, c * 1024:(c + 1) * 1024],
                s_wv[:, c * 1024:(c + 1) * 1024], s_gkcol[:, 6 + c:7 + c]), "dve")
        e_csb1 = None
        self.wait(T, "pe", e_wcq, self.now("dve"))
        for half in range(2):
            for c in range(16):
                ii = T.matmul(
                    p_cq[half][:, :], s_onesq[:, :],
                    s_wcq[:, c * QR + half * 384:c * QR + half * 384 + 384],
                    start=(c == 0), stop=(c == 15))
            e_mm = self.inc(ii, "pe")
            self.wait(Ae, "act", e_mm)
            e_csb1 = self.inc(Ae.copy(s_csbcq[:, half * 384:half * 384 + 384],
                                      p_cq[half][:, :]), "act")
        self.wait(T, "pe", e_wkvp)
        for c in range(16):
            ii = T.matmul(p_ckv[0][:, :], s_onesq[:, :],
                          s_wkv[:, c * 288:(c + 1) * 288],
                          start=(c == 0), stop=(c == 15))
        e_mm = self.inc(ii, "pe")
        self.wait(Ae, "act", e_mm)
        e_csb2 = self.inc(Ae.copy(s_csbkv[:, :], p_ckv[0][:, :]), "act")
        self.wait(T, "pe", e_ebr, self.now("act"))
        e_mm = self.inc(T.matmul(p_cq[0][:, 0:8], s_ones1[0:1, 0:128],
                                 s_ebr[:, :], start=True, stop=True), "pe")
        self.wait(Ae, "act", e_mm)
        e_ebb = self.inc(Ae.copy(s_ebb[:, :], p_cq[0][:, 0:8]), "act")
        e_kz = self.inc(V.memset(s_kasm[64:128, :], 0.0), "dve")

        ev_xt, ev_xf = [None, None], [None, None]
        last_rd_xt, last_rd_xf = [None, None], [None, None]
        last_rd_ckv = [None, None]
        e_ckv_wr = [None, None]
        prev_evac = None
        prev_rope = None
        e_nkv_ev = e_kro_ev = None
        for t in range(PFXT):
            bi = t % 2
            self.wait(SY, "sy", last_rd_xt[bi])
            ev_xt[bi] = self.dma(
                SY, s_xt[bi][:, :], d_xp_tm[128 * t:128 * t + 128, :],
                f"xt{bi}")
            self.wait(G, "gp", last_rd_xf[bi])
            ev_xf[bi] = self.dma(
                G, self.cpn(s_xf[bi], 16, 128),
                d_xp_fm[:, 128 * t:128 * t + 128].rearrange(
                    "(c p) k -> p c k", p=128), f"xf{bi}")
            e_st = self.ln_stats(s_xt[bi][:, :], D, s_sq[:, :], s_cp, 0, eps5,
                                 [ev_xt[bi]])
            last_rd_xt[bi] = e_st
            self.wait(T, "pe", ev_xf[bi], e_wkvp if t == 0 else None,
                      e_ckv_wr[bi])
            for c in range(16):
                ii = T.matmul(p_ckv[bi][:, :], s_xf[bi][:, c * 128:(c + 1) * 128],
                              s_wkv[:, c * 288:(c + 1) * 288],
                              start=(c == 0), stop=(c == 15))
            e_mm = self.inc(ii, "pe")
            last_rd_xf[bi] = e_mm
            self.wait(V, "dve", e_st, e_mm, e_csb2 if t == 0 else None,
                      last_rd_ckv[bi])
            e = self.inc(V.tensor_scalar_mul(s_ckv[bi][:, :], s_csbkv[:, :],
                                             s_cp[:, 1:2]), "dve")
            self.wait(V, "dve", e)
            e_corr = self.inc(V.scalar_tensor_tensor(
                s_ckv[bi][:, :], p_ckv[bi][:, :], s_cp[:, 0:1], s_ckv[bi][:, :],
                ALU.mult, ALU.add), "dve")
            e_ckv_wr[bi] = e_corr
            self.wait(Ae, "act", e_corr)
            e_ssq = self.inc(Ae.activation(s_sq[:, 0:256], s_ckv[bi][:, 0:256],
                                           AF.Square, accum_out=s_cp[:, 7:8]),
                             "act")
            self.wait(V, "dve", e_ssq)
            e = self.inc(V.tensor_scalar_mul(s_cp[:, 7:8], s_cp[:, 7:8],
                                             1.0 / KVR), "dve")
            self.wait(Ae, "act", e)
            e = self.inc(Ae.activation(s_cp[:, 6:7], s_cp[:, 7:8], AF.Ln,
                                       bias=eps6), "act")
            self.wait(Ae, "act", e)
            e_rr = self.inc(Ae.activation(s_cp[:, 6:7], s_cp[:, 6:7], AF.Exp,
                                          scale=-0.5), "act")
            self.wait(Ae, "act", e_rr, e_nkv_ev)
            e_nkvap = self.inc(Ae.activation(s_nqtm[:, 0:256], s_ckv[bi][:, 0:256],
                                             AF.Identity, bias=0.0,
                                             scale=s_cp[:, 6:7]), "act")
            e_nkv_ev = e_nkvap
            cosr = bass.AP(s_cosk, 16 * t, [[PFXT * 16, 128], [0, 2], [1, 16]])
            self.wait(V, "dve", e_corr, e_cosk, e_kro_ev)
            e_kro = self.inc(V.tensor_tensor(s_kro[:, :], s_ckv[bi][:, 256:288],
                                             cosr, ALU.mult), "dve")
            e_kro_ev = e_kro
            last_rd_ckv[bi] = e_kro
            self.wait(T, "pe", e_nkvap, e_kro, e_ident if t == 0 else None,
                      prev_evac)
            T.transpose(p_trn[:, 0:128], s_nqtm[:, 0:128], s_ident[:, :])
            T.transpose(p_trn[:, 128:256], s_nqtm[:, 128:256], s_ident[:, :])
            ii = T.transpose(p_trk[0:32, :], s_kro[:, :], s_ident[:, :])
            e_tr = self.inc(ii, "pe")
            self.wait(Ae, "act", e_tr)
            Ae.copy(s_nkv[:, 128 * t:128 * t + 128], p_trn[:, 0:128])
            Ae.copy(s_nkv[:, PFX + 128 * t:PFX + 128 * t + 128],
                    p_trn[:, 128:256])
            self.wait(Ae, "act", e_kz if t == 0 else None)
            prev_evac = self.inc(Ae.copy(s_kasm[64:96, 128 * t:128 * t + 128],
                                         p_trk[0:32, :]), "act")
            e_nkv_ev = prev_evac
            e_kro_ev = prev_evac
        self.tap("nkv", s_nkv[:, :], [128, 2 * PFX], BF16)
        self.tap("kasm", s_kasm[64:128, :], [64, PFX], BF16)
        if self.stop_after < 2:
            return self.finish()

        e_nq_ev = None
        e_cq_wr = None
        for tt in range(NPT):
            bi = tt % 2
            self.wait(SY, "sy", last_rd_xt[bi])
            ev_xt[bi] = self.dma(
                SY, s_xt[bi][:, :], d_own_tm[128 * tt:128 * tt + 128, :],
                f"xt{bi}")
            self.wait(G, "gp", last_rd_xf[bi])
            ev_xf[bi] = self.dma(
                G, self.cpn(s_xf[bi], 16, 128),
                d_own_fm[:, 128 * tt:128 * tt + 128].rearrange(
                    "(c p) k -> p c k", p=128), f"xf{bi}")
            e_st = self.ln_stats(s_xt[bi][:, :], D, s_sq[:, :], s_c1, 2 * NPT,
                                 eps5, [ev_xt[bi]])
            last_rd_xt[bi] = e_st
            self.wait(V, "dve", e_st)
            e = self.inc(V.tensor_copy(s_c1[:, tt:tt + 1],
                                       s_c1[:, 2 * NPT:2 * NPT + 1]), "dve")
            self.wait(V, "dve", e)
            e_cp = self.inc(V.tensor_copy(s_c1[:, NPT + tt:NPT + tt + 1],
                                          s_c1[:, 2 * NPT + 1:2 * NPT + 2]),
                            "dve")
            self.wait(T, "pe", ev_xf[bi], e_wcq if tt == 0 else None, e_cq_wr)
            for half in range(2):
                for c in range(16):
                    ii = T.matmul(
                        p_cq[half][:, :], s_xf[bi][:, c * 128:(c + 1) * 128],
                        s_wcq[:, c * QR + half * 384:c * QR + half * 384 + 384],
                        start=(c == 0), stop=(c == 15))
            e_mm = self.inc(ii, "pe")
            last_rd_xf[bi] = e_mm
            self.wait(V, "dve", e_mm, e_cp, e_csb1 if tt == 0 else None)
            e = self.inc(V.tensor_scalar_mul(
                s_t768[:, :], s_csbcq[:, :], s_c1[:, NPT + tt:NPT + tt + 1]),
                "dve")
            self.wait(V, "dve", e)
            e = self.inc(V.scalar_tensor_tensor(
                s_cq[:, 0:384], p_cq[0][:, :], s_c1[:, tt:tt + 1],
                s_t768[:, 0:384], ALU.mult, ALU.add), "dve")
            self.wait(V, "dve", e)
            e_corr = self.inc(V.scalar_tensor_tensor(
                s_cq[:, 384:768], p_cq[1][:, :], s_c1[:, tt:tt + 1],
                s_t768[:, 384:768], ALU.mult, ALU.add), "dve")
            e_cq_wr = e_corr
            e_rr = self.ln_stats(s_cq[:, :], QR, s_sq[:, 0:QR], s_cg, 0, eps6,
                                 [e_corr], rms=True)
            self.wait(Ae, "act", e_rr, e_nq_ev)
            e_ap = self.inc(Ae.activation(s_nqtm[:, :], s_cq[:, :], AF.Identity,
                                          bias=0.0, scale=s_cg[:, 0:1]), "act")
            self.wait(T, "pe", e_ap)
            for c in range(6):
                ii = T.transpose(p_trq[:, c * 128:(c + 1) * 128],
                                 s_nqtm[:, c * 128:(c + 1) * 128], s_ident[:, :])
            e_tr = self.inc(ii, "pe")
            dst = bass.AP(s_nq, 128 * tt, [[6 * NT, 128], [NT, 6], [1, 128]])
            self.wait(Ae, "act", e_tr)
            e_nq_ev = self.inc(Ae.copy(dst, p_trq[:, :]), "act")
        self.tap("nq", s_nq[:, :], [128, 6 * NT], BF16)
        self.close(psA1)
        self.close(esA1)
        if self.stop_after < 3:
            return self.finish()

        esM2 = self.scope()
        s_ln2 = self.sb(esM2, "ln2fm", [128, 16 * NT], BF16, side="right")
        esM1 = self.scope()
        s_ao = self.sb(esM1, "attnout", [128, 8 * NT], BF16, side="right")
        s_wproj = self.sb(esM1, "sb_wproj", [128, 8 * D], BF16, side="right")

        psAT = self.scope()
        p_s = [self.ps(psAT, f"p_s{i}", [128, 512]) for i in range(2)]
        p_av = [self.ps(psAT, f"p_av{i}", [128, 512]) for i in range(2)]
        p_bc = self.ps(psAT, "p_bc", [64, 512])

        self.barrier()
        e_wproj = self.dma(G, self.cpn(s_wproj, 8, D), rr(d_wproj))

        ev_pe_last_score = None
        ev_norm = [None, None]
        e_vev = [None, None]
        e_kev = [None, None]
        e_kev_last = None
        e_q_done = None
        prev_bcm = prev_rbv = prev_nm = None
        for hg in range(4):
            ones_pat = bass.AP(s_vgrp, 64, [[260 * PFXT, 128], [260, PFXT],
                                            [65, 4], [1, 1]])
            self.wait(V, "dve", self.now("pe"))
            e_ones = self.inc(V.memset(ones_pat, 1.0), "dve")
            for t in range(PFXT):
                bi = t % 2
                self.wait(T, "pe", e_wvp if hg == 0 and t == 0 else None,
                          ev_norm[bi] if t < 2 else None,
                          e_vev[bi])
                for rc in range(2):
                    ii = T.matmul(
                        p_av[bi][:, 0:256],
                        s_nkv[:, rc * PFX + 128 * t:rc * PFX + 128 * t + 128],
                        s_wv[:, rc * 1024 + 256 * hg:rc * 1024 + 256 * hg + 256],
                        start=(rc == 0), stop=(rc == 1))
                e_mm = self.inc(ii, "pe")
                dst = bass.AP(s_vgrp, 260 * t, [[260 * PFXT, 128], [65, 4],
                                                [1, 64]])
                self.wait(V, "dve", e_mm, e_ones)
                e_vev[bi] = self.inc(V.tensor_copy(dst, p_av[bi][:, 0:256]),
                                     "dve")
            for hj in range(4):
                h = 4 * hg + hj
                for kc in range(NKC):
                    bi = kc % 2
                    self.wait(T, "pe",
                              e_wknp if h == 0 and kc == 0 else None,
                              e_kev[bi])
                    for rc in range(2):
                        ii = T.matmul(
                            p_s[bi][0:64, :],
                            s_wkn[:, rc * 1024 + ND * h:rc * 1024 + ND * h + ND],
                            s_nkv[:, rc * PFX + 512 * kc:
                                  rc * PFX + 512 * kc + 512],
                            start=(rc == 0), stop=(rc == 1))
                    e_mm = self.inc(ii, "pe")
                    self.wait(Ae, "act", e_mm, ev_pe_last_score)
                    e_kev[bi] = self.inc(
                        Ae.copy(s_kasm[0:64, 512 * kc:512 * kc + 512],
                                p_s[bi][0:64, :]), "act")
                e_kev_last = e_kev[(NKC - 1) % 2]
                e_cm_q = [None, None]
                for qc in range(2):
                    self.wait(T, "pe", e_wqp if h == 0 and qc == 0 else None,
                              ev_norm[qc], e_vev[qc])
                    for r in range(6):
                        ii = T.matmul(
                            p_av[qc][:, :],
                            s_wq[:, r * 2048 + 128 * h:r * 2048 + 128 * h + 128],
                            s_nq[:, r * NT + 512 * qc:r * NT + 512 * qc + 512],
                            start=(r == 0), stop=(r == 5))
                    e_mm = self.inc(ii, "pe")
                    self.wait(V, "dve", e_mm,
                              e_cosq if h == 0 and qc == 0 else None)
                    e_cm = self.inc(V.tensor_tensor(
                        p_av[qc][64:96, :], p_av[qc][64:96, :],
                        s_cosq[64:96, 512 * qc:512 * qc + 512], ALU.mult), "dve")
                    e_cm_q[qc] = e_cm
                    self.wait(Ae, "act", e_cm)
                    e_q_done = self.inc(
                        Ae.copy(s_qh[:, 512 * qc:512 * qc + 512],
                                p_av[qc][:, :]), "act")
                for ph, (tiles, qc, bias_s, diag0) in enumerate(
                        [(A_TILES, 0, s_biasa, 12), (B_TILES, 1, s_biasb, 28)]):
                    n_it = len(tiles)
                    q0 = 512 * qc
                    ev_exp = [None] * n_it
                    ev_sc = [None] * n_it

                    def col0_of(i):
                        return 128 * (i - diag0) if i >= diag0 else 0

                    def emit_score(i):
                        t = tiles[i]
                        c0 = col0_of(i)
                        masked = i >= diag0
                        self.wait(T, "pe",
                                  e_kev_last if i == 0 else None,
                                  e_q_done if i == 0 else None,
                                  ev_exp[i - 2] if i >= 2 else None)
                        ii = T.matmul(p_s[i % 2][:, c0:512],
                                      s_kasm[:, 128 * t:128 * t + 128],
                                      s_qh[:, q0 + c0:q0 + 512],
                                      start=True, stop=not masked)
                        if masked:
                            ri = i - diag0
                            self.wait(T, "pe",
                                      e_rbm if h == 0 and ph == 0 else None)
                            ii = T.matmul(p_s[i % 2][:, c0:512], s_ident[:, :],
                                          s_rbm[:, 512 * ri + c0:512 * ri + 512],
                                          start=False, stop=True)
                        ev_sc[i] = self.inc(ii, "pe")

                    def emit_exp(i):
                        c0 = col0_of(i)
                        self.wait(Ae, "act", ev_sc[i],
                                  (e_ba if ph == 0 else e_bb) if h == 0 else None)
                        ev_exp[i] = self.inc(Ae.activation(
                            s_pb[i % 2][:, c0:512], p_s[i % 2][:, c0:512],
                            AF.Exp, bias=bias_s[:, i:i + 1], scale=SCALE), "act")

                    def emit_av(i):
                        t = tiles[i]
                        c0 = col0_of(i)
                        self.wait(T, "pe", ev_exp[i],
                                  ev_norm[qc] if i == 0 else None,
                                  e_cm_q[qc] if i == 0 else None,
                                  e_vev[0] if i == 0 else None,
                                  e_vev[1] if i == 0 else None)
                        ii = T.matmul(p_av[qc][0:65, c0:512],
                                      s_vgrp[:, 260 * t + 65 * hj:
                                             260 * t + 65 * hj + 65],
                                      s_pb[i % 2][:, c0:512],
                                      start=(i == 0), stop=(i == n_it - 1))
                        if i == n_it - 1:
                            return self.inc(ii, "pe")
                        return None

                    emit_score(0)
                    emit_exp(0)
                    emit_score(1)
                    emit_exp(1)
                    e_av_done = None
                    for i in range(n_it):
                        r = emit_av(i)
                        if r is not None:
                            e_av_done = r
                        if i + 2 < n_it:
                            emit_score(i + 2)
                            emit_exp(i + 2)
                    ev_pe_last_score = ev_sc[n_it - 1]
                    self.wait(V, "dve", e_av_done, prev_bcm)
                    e_rec = self.inc(V.reciprocal(s_row[:, :],
                                                  p_av[qc][64:65, :]), "dve")
                    self.wait(T, "pe", e_rec, prev_rbv)
                    e_bcm = self.inc(T.matmul(p_bc[:, :], s_ones1[0:1, 0:64],
                                              s_row[:, :], start=True,
                                              stop=True), "pe")
                    prev_bcm = e_bcm
                    self.wait(Ae, "act", e_bcm, prev_nm)
                    e_rbv = self.inc(Ae.copy(s_rb[:, :], p_bc[:, :]), "act")
                    prev_rbv = e_rbv
                    ao_off = (h // 2) * NT + q0
                    ao_p0 = 64 * (h % 2)
                    self.wait(V, "dve", e_rbv)
                    e_nm = self.inc(V.tensor_mul(
                        s_ao[ao_p0:ao_p0 + 64, ao_off:ao_off + 512],
                        p_av[qc][0:64, :], s_rb[:, :]), "dve")
                    prev_nm = e_nm
                    ev_norm[qc] = e_nm
        self.tap("attnout", s_ao[:, :], [128, 8 * NT], BF16)
        self.close(psAT)
        self.close(esAO)
        if self.stop_after < 4:
            return self.finish()

        psM = self.scope()
        p_proj = self.ps(psM, "p_proj", [128, 2048])
        p_ltr = self.ps(psM, "p_ltr", [128, 2048], BF16)
        esP4 = self.scope()
        s_xo = [self.sb(esP4, f"xo{i}", [128, D], F32) for i in range(2)]
        s_lnap = [self.sb(esP4, f"lnap{i}", [128, D], F32) for i in range(2)]
        s_ln2tm = self.sb(esP4, "ln2tm", [128, D], BF16)
        s_sq4 = self.sb(esP4, "sq4", [128, D], BF16)

        self.barrier()

        ev_xo = [None, None]
        last_rd_xo = [[], []]
        last_rd_lnap = [None, None]
        e_ltr_ev = None
        e_proj_rd = None
        e_x2st = [None] * NPT
        for tt in range(NPT):
            bi = tt % 2
            self.wait(SY, "sy", *last_rd_xo[bi])
            ev_xo[bi] = self.dma(
                SY, s_xo[bi][:, :], d_own_tm[128 * tt:128 * tt + 128, :],
                f"xo{bi}")
            self.wait(T, "pe", e_wproj if tt == 0 else None, e_proj_rd)
            for cs in range(4):
                for c in range(8):
                    ii = T.matmul(
                        p_proj[:, 512 * cs:512 * cs + 512],
                        s_ao[:, c * NT + 128 * tt:c * NT + 128 * tt + 128],
                        s_wproj[:, c * D + 512 * cs:c * D + 512 * cs + 512],
                        start=(c == 0), stop=(c == 7))
            e_mm = self.inc(ii, "pe")
            if self.p4sub < 1:
                self.wait(Ae, "act", e_mm, last_rd_lnap[bi])
                e_ap0 = self.inc(Ae.copy(s_lnap[bi][:, :], p_proj[:, :]), "act")
                e_proj_rd = e_ap0
                last_rd_lnap[bi] = e_ap0
                last_rd_xo[bi] = [ev_xo[bi]]
                continue
            self.wait(Ae, "act", e_mm, last_rd_lnap[bi])
            e_cp0 = self.inc(Ae.copy(s_lnap[bi][:, :], p_proj[:, :]), "act")
            e_proj_rd = e_cp0
            e_st = self.ln_stats(s_lnap[bi][:, :], D, s_sq4[:, :], s_cg, 0,
                                 eps5, [e_cp0])
            self.wait(Ae, "act", e_st)
            e_ap = self.inc(Ae.activation(s_lnap[bi][:, :], s_lnap[bi][:, :],
                                          AF.Identity, bias=s_cg[:, 1:2],
                                          scale=s_cg[:, 0:1]), "act")
            self.wait(V, "dve", e_ap, ev_xo[bi])
            e_x2 = self.inc(V.tensor_add(s_xo[bi][:, :], s_xo[bi][:, :],
                                         s_lnap[bi][:, :]), "dve")
            last_rd_lnap[bi] = e_x2
            self.wait(SY, "sy", e_x2)
            e_x2st[tt] = self.dma(
                SY, d_x2[128 * tt:128 * tt + 128, :], s_xo[bi][:, :],
                f"x2s{bi}")
            if self.p4sub < 2:
                last_rd_xo[bi] = [e_x2, e_x2st[tt]]
                last_rd_lnap[bi] = e_x2
                continue
            e_st2 = self.ln_stats(s_xo[bi][:, :], D, s_sq4[:, :], s_c2, 2 * NPT,
                                  eps5, [e_x2])
            self.wait(V, "dve", e_st2)
            e = self.inc(V.tensor_copy(s_c2[:, tt:tt + 1],
                                       s_c2[:, 2 * NPT:2 * NPT + 1]), "dve")
            self.wait(V, "dve", e)
            e_cp = self.inc(V.tensor_copy(s_c2[:, NPT + tt:NPT + tt + 1],
                                          s_c2[:, 2 * NPT + 1:2 * NPT + 2]),
                            "dve")
            self.wait(Ae, "act", e_cp, e_ltr_ev)
            e_ap2 = self.inc(Ae.activation(s_ln2tm[:, :], s_xo[bi][:, :],
                                           AF.Identity,
                                           bias=s_c2[:, NPT + tt:NPT + tt + 1],
                                           scale=s_c2[:, tt:tt + 1]), "act")
            last_rd_xo[bi] = [e_ap2, e_x2st[tt]]
            if self.p4sub < 3:
                continue
            self.wait(T, "pe", e_ap2, e_ltr_ev)
            for c in range(16):
                ii = T.transpose(p_ltr[:, c * 128:(c + 1) * 128],
                                 s_ln2tm[:, c * 128:(c + 1) * 128],
                                 s_ident[:, :])
            e_tr = self.inc(ii, "pe")
            self.wait(Ae, "act", e_tr)
            for hh in range(2):
                dst = bass.AP(s_ln2, (8 * hh) * NT + 128 * tt,
                              [[16 * NT, 128], [NT, 8], [1, 128]])
                e_ltr_ev = self.inc(
                    Ae.copy(dst, p_ltr[:, hh * 1024:hh * 1024 + 1024]), "act")
        self.tap("ln2fm", s_ln2[:, :], [128, 16 * NT], BF16)
        self.close(psM)
        self.close(esP4)
        self.close(esM1)
        if self.stop_after < 5:
            return self.finish()

        esF = self.scope()
        s_hid = self.sb(esF, "hidden", [128, NDFF * 512], BF16)
        s_facc = self.sb(esF, "facc", [128, 4 * D], F32)
        s_wra = self.sb(esF, "sb_wra", [128, 16 * 152], BF16)
        s_adc = self.sb(esF, "sb_adc", [128, NDFF * 72], BF16)
        s_bgc = self.sb(esF, "sb_bgc", [72, DFF], BF16)
        s_buc = self.sb(esF, "sb_buc", [72, DFF], BF16)
        s_bdc = self.sb(esF, "sb_bdc", [72, D], BF16)
        s_wgb = [self.sb(esF, f"wgb{i}", [128, 2048], BF16) for i in range(2)]
        s_wub = [self.sb(esF, f"wub{i}", [128, 2048], BF16) for i in range(2)]
        s_wdb = [self.sb(esF, f"wdb{i}", [128, 512], BF16) for i in range(4)]
        s_xaw = self.sb(esF, "xaw", [128, 144], BF16)
        s_haw = self.sb(esF, "haw", [128, 72], BF16)
        s_xawg = self.sb(esF, "xawg", [72, 512], BF16)
        s_xawu = self.sb(esF, "xawu", [72, 512], BF16)
        s_hawd = self.sb(esF, "hawd", [72, 512], BF16)
        s_sg = self.sb(esF, "sgate", [128, 512], BF16)
        s_rc = self.sb(esF, "rcols", [128, 32], F32)
        s_xo2 = [self.sb(esF, f"xo2{i}", [128, D], F32) for i in range(2)]
        s_fap = [self.sb(esF, f"fap{i}", [128, D], F32) for i in range(2)]

        psF = self.scope()
        p_g = self.ps(psF, "p_g", [128, 512])
        p_u = self.ps(psF, "p_u", [128, 512])
        p_xa = self.ps(psF, "p_xa", [128, 152])
        p_t72 = self.ps(psF, "p_t72", [72, 256], BF16)
        p_d = [self.ps(psF, f"p_d{i}", [128, 512]) for i in range(4)]

        self.barrier()

        e_wra = self.dma(G, self.cpn(s_wra, 16, 152), rr(d_wra))
        e_adc = self.dma(G, self.cpn(s_adc, NDFF, 72), rr(d_adc))
        e_bgc = self.dma(G, s_bgc[:, :], d_bgc[:, :])
        e_buc = self.dma(G, s_buc[:, :], d_buc[:, :])
        e_bdc = self.dma(G, s_bdc[:, :], d_bdc[:, :])

        ev_wg = [None, None]
        rd_wg = [None, None]
        ev_wu = [None, None]
        rd_wu = [None, None]
        ev_wd = [None] * 4
        rd_wd = [None] * 4
        e_si = None
        e_mul = None
        e_xa_rd = None
        e_xa_rd_act = None
        e_t72_rd = None
        e_down_mm_last = None
        e_ep_dve_last = None
        last_rd_xo2 = [[], []]
        last_wr_fap = [None, None]
        for half in range(2):
            q0 = 512 * half
            e_xaw_tr = []
            for tt in range(4):
                gt = 4 * half + tt
                self.wait(T, "pe", e_wra if gt == 0 else None, e_xa_rd,
                          e_xa_rd_act)
                for c in range(16):
                    ii = T.matmul(
                        p_xa[:, :],
                        s_ln2[:, c * NT + 128 * gt:c * NT + 128 * gt + 128],
                        s_wra[:, c * 152:(c + 1) * 152],
                        start=(c == 0), stop=(c == 15))
                e_mm = self.inc(ii, "pe")
                r = s_rc
                self.wait(Ae, "act", e_mm)
                e_sig = self.inc(Ae.activation(r[:, 0:8], p_xa[:, 0:8],
                                               AF.Sigmoid), "act")
                e_xa_rd_act = e_sig
                self.wait(V, "dve", e_sig, e_ebb if gt == 0 else None)
                e = self.inc(V.tensor_add(r[:, 8:16], r[:, 0:8], s_ebb[:, :]),
                             "dve")
                self.wait(V, "dve", e)
                e = self.inc(V.tensor_reduce(r[:, 24:25], r[:, 8:16], AX.X,
                                             ALU.max), "dve")
                self.wait(V, "dve", e)
                e = self.inc(V.tensor_scalar(r[:, 16:24], r[:, 8:16],
                                             r[:, 24:25], None, ALU.is_equal),
                             "dve")
                self.wait(V, "dve", e)
                e = self.inc(V.scalar_tensor_tensor(r[:, 16:24], r[:, 16:24],
                                                    NEG, r[:, 8:16], ALU.mult,
                                                    ALU.add), "dve")
                self.wait(V, "dve", e)
                e = self.inc(V.tensor_reduce(r[:, 25:26], r[:, 16:24], AX.X,
                                             ALU.max), "dve")
                self.wait(V, "dve", e)
                e = self.inc(V.tensor_scalar(r[:, 16:24], r[:, 8:16],
                                             r[:, 25:26], None, ALU.is_ge),
                             "dve")
                self.wait(V, "dve", e)
                e = self.inc(V.tensor_mul(r[:, 16:24], r[:, 16:24], r[:, 0:8]),
                             "dve")
                self.wait(V, "dve", e)
                e = self.inc(V.tensor_reduce(r[:, 26:27], r[:, 16:24], AX.X,
                                             ALU.add), "dve")
                self.wait(V, "dve", e)
                e = self.inc(V.tensor_scalar_add(r[:, 26:27], r[:, 26:27],
                                                 1e-8), "dve")
                self.wait(V, "dve", e)
                e = self.inc(V.reciprocal(r[:, 27:28], r[:, 26:27]), "dve")
                self.wait(V, "dve", e)
                e_wx = self.inc(V.tensor_scalar(
                    s_wx[:, 8 * gt:8 * gt + 8], r[:, 16:24], r[:, 27:28],
                    0.125, ALU.mult, ALU.mult), "dve")
                wrep = bass.AP(s_wx, 8 * gt, [[8 * NPT, 128], [1, 8], [0, 8]])
                self.wait(V, "dve", e_wx)
                self.inc(V.tensor_scalar_mul(s_xaw[:, 0:8], p_xa[:, 8:16],
                                             0.125), "dve")
                self.inc(V.tensor_tensor(s_xaw[:, 8:72], p_xa[:, 16:80], wrep,
                                         ALU.mult), "dve")
                self.inc(V.tensor_scalar_mul(s_xaw[:, 72:80], p_xa[:, 80:88],
                                             0.125), "dve")
                e_xa_rd = self.inc(V.tensor_tensor(s_xaw[:, 80:144],
                                                   p_xa[:, 88:152], wrep,
                                                   ALU.mult), "dve")
                self.wait(T, "pe", e_xa_rd, e_t72_rd)
                T.transpose(p_t72[0:72, 0:128], s_xaw[:, 0:72], s_ident[:, :])
                ii = T.transpose(p_t72[0:72, 128:256], s_xaw[:, 72:144],
                                 s_ident[:, :])
                e_tr = self.inc(ii, "pe")
                self.wait(Ae, "act", e_tr, e_down_mm_last)
                Ae.copy(s_xawg[0:72, 128 * tt:128 * tt + 128],
                        p_t72[0:72, 0:128])
                e_t72_rd = self.inc(
                    Ae.copy(s_xawu[0:72, 128 * tt:128 * tt + 128],
                            p_t72[0:72, 128:256]), "act")
                e_xaw_tr.append(e_t72_rd)
            for j in range(NDFF):
                bi = j % 2
                self.wait(G, "gp", rd_wg[bi])
                ev_wg[bi] = self.dma(
                    G, self.cpn(s_wgb[bi], 16, 128),
                    d_wg[:, 128 * j:128 * j + 128].rearrange(
                        "(c p) n -> p c n", p=128), f"wg{bi}")
                self.wait(G, "gp", rd_wu[bi])
                ev_wu[bi] = self.dma(
                    G, self.cpn(s_wub[bi], 16, 128),
                    d_wu[:, 128 * j:128 * j + 128].rearrange(
                        "(c p) n -> p c n", p=128), f"wu{bi}")
                self.wait(T, "pe", ev_wg[bi], e_xaw_tr[-1] if j == 0 else None,
                          e_bgc if half == 0 and j == 0 else None, e_si)
                for c in range(16):
                    T.matmul(p_g[:, :], s_wgb[bi][:, c * 128:(c + 1) * 128],
                             s_ln2[:, c * NT + q0:c * NT + q0 + 512],
                             start=(c == 0), stop=False)
                e_gmm = self.inc(T.matmul(p_g[:, :], s_bgc[0:72, 128 * j:
                                                           128 * j + 128],
                                          s_xawg[0:72, :], start=False,
                                          stop=True), "pe")
                rd_wg[bi] = e_gmm
                self.wait(T, "pe", ev_wu[bi],
                          e_buc if half == 0 and j == 0 else None, e_mul)
                for c in range(16):
                    T.matmul(p_u[:, :], s_wub[bi][:, c * 128:(c + 1) * 128],
                             s_ln2[:, c * NT + q0:c * NT + q0 + 512],
                             start=(c == 0), stop=False)
                e_umm = self.inc(T.matmul(p_u[:, :], s_buc[0:72, 128 * j:
                                                           128 * j + 128],
                                          s_xawu[0:72, :], start=False,
                                          stop=True), "pe")
                rd_wu[bi] = e_umm
                self.wait(Ae, "act", e_gmm, e_mul)
                e_si = self.inc(Ae.activation(s_sg[:, :], p_g[:, :], AF.Silu),
                                "act")
                self.wait(V, "dve", e_si, e_umm,
                          e_down_mm_last if j == 0 else None,
                          e_ep_dve_last if j < 4 else None)
                e_mul = self.inc(V.tensor_mul(
                    s_hid[:, 512 * j:512 * j + 512], s_sg[:, :], p_u[:, :]),
                    "dve")
            e_hid_done = e_mul
            e_haw_tr = []
            for tt in range(4):
                gt = 4 * half + tt
                self.wait(T, "pe", e_hid_done if tt == 0 else None,
                          e_adc if half == 0 and tt == 0 else None,
                          e_xa_rd, e_xa_rd_act)
                for j in range(NDFF):
                    ii = T.matmul(
                        p_xa[:, 0:72],
                        s_hid[:, 512 * j + 128 * tt:512 * j + 128 * tt + 128],
                        s_adc[:, 72 * j:72 * j + 72],
                        start=(j == 0), stop=(j == NDFF - 1))
                e_mm = self.inc(ii, "pe")
                wrep = bass.AP(s_wx, 8 * gt, [[8 * NPT, 128], [1, 8], [0, 8]])
                self.wait(V, "dve", e_mm)
                self.inc(V.tensor_scalar_mul(s_haw[:, 0:8], p_xa[:, 0:8],
                                             0.125), "dve")
                e_xa_rd = self.inc(V.tensor_tensor(s_haw[:, 8:72],
                                                   p_xa[:, 8:72], wrep,
                                                   ALU.mult), "dve")
                self.wait(T, "pe", e_xa_rd, e_t72_rd)
                ii = T.transpose(p_t72[0:72, 0:128], s_haw[:, 0:72],
                                 s_ident[:, :])
                e_tr = self.inc(ii, "pe")
                self.wait(Ae, "act", e_tr, e_down_mm_last)
                e_t72_rd = self.inc(
                    Ae.copy(s_hawd[0:72, 128 * tt:128 * tt + 128],
                            p_t72[0:72, 0:128]), "act")
                e_haw_tr.append(e_t72_rd)
            e_dev = [None] * 4
            for cq in range(4):
                for j in range(NDFF):
                    bi = j % 4
                    self.wait(G, "gp", rd_wd[bi])
                    ev_wd[bi] = self.dma(
                        G, s_wdb[bi][:, :],
                        d_wd[128 * j:128 * j + 128, 512 * cq:512 * cq + 512],
                        f"wd{bi}")
                    if j == 0:
                        self.wait(T, "pe", *e_dev)
                        self.wait(T, "pe", e_ep_dve_last)
                    self.wait(T, "pe", ev_wd[bi])
                    for tt in range(4):
                        ii = T.matmul(
                            p_d[tt][:, :],
                            s_hid[:, 512 * j + 128 * tt:
                                  512 * j + 128 * tt + 128],
                            s_wdb[bi][:, :],
                            start=(j == 0), stop=False)
                    rd_wd[bi] = self.inc(ii, "pe")
                self.wait(T, "pe", e_haw_tr[-1] if cq == 0 else None,
                          e_bdc if half == 0 and cq == 0 else None)
                for tt in range(4):
                    e_bmm = self.inc(T.matmul(
                        p_d[tt][:, :], s_hawd[0:72, 128 * tt:128 * tt + 128],
                        s_bdc[0:72, 512 * cq:512 * cq + 512],
                        start=False, stop=True), "pe")
                    e_down_mm_last = e_bmm
                    self.wait(Ae, "act", e_bmm,
                              e_ep_dve_last if cq == 0 else None)
                    e_dev[tt] = self.inc(
                        Ae.copy(s_facc[:, 2048 * tt + 512 * cq:
                                       2048 * tt + 512 * cq + 512],
                                p_d[tt][:, :]), "act")
            for tt in range(4):
                gt = 4 * half + tt
                bi = tt % 2
                e_st = self.ln_stats(s_facc[:, 2048 * tt:2048 * tt + 2048], D,
                                     s_hid[:, 0:2048], s_cg, 0, eps5,
                                     [e_dev[tt], rd_wd[0], rd_wd[1],
                                      rd_wd[2], rd_wd[3]])
                e_ep_dve_last = e_st
                self.wait(SY, "sy", e_x2st[gt], *last_rd_xo2[bi])
                ev_x2 = self.dma(
                    SY, s_xo2[bi][:, :], d_x2[128 * gt:128 * gt + 128, :],
                    f"xo2_{bi}")
                self.wait(Ae, "act", e_st, last_wr_fap[bi])
                e_ap = self.inc(Ae.activation(
                    s_fap[bi][:, :], s_facc[:, 2048 * tt:2048 * tt + 2048],
                    AF.Identity, bias=s_cg[:, 1:2], scale=s_cg[:, 0:1]), "act")
                self.wait(V, "dve", e_ap, ev_x2)
                e_fin = self.inc(V.tensor_add(s_xo2[bi][:, :], s_xo2[bi][:, :],
                                              s_fap[bi][:, :]), "dve")
                last_wr_fap[bi] = e_fin
                self.wait(SY, "sy", e_fin)
                e_ost = self.dma(
                    SY, d_out[128 * gt:128 * gt + 128, :], s_xo2[bi][:, :],
                    f"outst{bi}")
                last_rd_xo2[bi] = [e_fin, e_ost]
        return self.finish()


_FREQS = (1.0 / 10000.0 ** (np.arange(0, RD, 2, dtype=np.float64) / RD))


def _cos_table(pos):
    ang = np.outer(pos.astype(np.float64), _FREQS)
    return np.cos(ang).astype(np.float32)


def _pack_shared(inp):
    f32 = lambda k: np.asarray(inp[k], dtype=np.float32)
    w_q_nope = f32("L0_attn_w_q_nope")
    w_q_rope = f32("L0_attn_w_q_rope")
    wq_cat = np.zeros((QR, H * 128), np.float32)
    for h in range(H):
        wq_cat[:, 128 * h:128 * h + 64] = w_q_nope[:, 64 * h:64 * h + 64]
        wq_cat[:, 128 * h + 64:128 * h + 80] = w_q_rope[:, 32 * h:32 * h + 32:2]
        wq_cat[:, 128 * h + 80:128 * h + 96] = w_q_rope[:, 32 * h + 1:
                                                        32 * h + 32:2]
    w_k_rope = f32("L0_attn_w_k_rope")
    wkv_cat = np.concatenate(
        [f32("L0_attn_w_ckv"), w_k_rope[:, 0::2], w_k_rope[:, 1::2]], axis=1)
    A_g = np.ascontiguousarray(f32("L0_ffn_A_gate").transpose(1, 0, 2)
                               ).reshape(D, 64)
    A_u = np.ascontiguousarray(f32("L0_ffn_A_up").transpose(1, 0, 2)
                               ).reshape(D, 64)
    w_ra = np.concatenate([f32("L0_ffn_W_router"), f32("L0_ffn_sA_gate"), A_g,
                           f32("L0_ffn_sA_up"), A_u], axis=1)
    bgc = np.concatenate([f32("L0_ffn_sB_gate"),
                          f32("L0_ffn_B_gate").reshape(64, DFF)], axis=0)
    buc = np.concatenate([f32("L0_ffn_sB_up"),
                          f32("L0_ffn_B_up").reshape(64, DFF)], axis=0)
    A_d = np.ascontiguousarray(f32("L0_ffn_A_down").transpose(1, 0, 2)
                               ).reshape(DFF, 64)
    adc = np.concatenate([f32("L0_ffn_sA_down"), A_d], axis=1)
    bdc = np.concatenate([f32("L0_ffn_sB_down"),
                          f32("L0_ffn_B_down").reshape(64, D)], axis=0)
    shared = {
        "ident": np.eye(128, dtype=BFNP),
        "rbm": _make_rbm(),
        "w_cq": f32("L0_attn_w_cq"),
        "wq_cat": wq_cat,
        "wkv_cat": np.ascontiguousarray(wkv_cat),
        "w_kn": f32("L0_attn_w_k_nope"),
        "w_v": f32("L0_attn_w_v"),
        "w_proj": f32("L0_attn_w_proj"),
        "w_ra": np.ascontiguousarray(w_ra),
        "bgc": np.ascontiguousarray(bgc),
        "buc": np.ascontiguousarray(buc),
        "adc": np.ascontiguousarray(adc),
        "bdc": np.ascontiguousarray(bdc),
        "wg": f32("L0_ffn_W_gate"),
        "wu": f32("L0_ffn_W_up"),
        "wd": f32("L0_ffn_W_down"),
        "gcol": np.ascontiguousarray(
            f32("L0_attn_q_g").reshape(6, 128).T),
        "kvcol": np.ascontiguousarray(
            f32("L0_attn_kv_g").reshape(2, 128).T),
        "eb": f32("L0_ffn_expert_bias").reshape(1, E),
    }
    return shared


def _make_rbm():
    rbm = np.zeros((4, 128, 512), np.float32)
    kk = np.arange(128)
    for i in range(4):
        for j in range(4):
            blk = rbm[i][:, 128 * j:128 * j + 128]
            if j < i:
                blk[:] = NEG
            elif j == i:
                blk[:] = np.where(kk[:, None] > kk[None, :], NEG, 0.0)
    return rbm.astype(BFNP)


def _pack_core(c, x_full):
    p = c % 4
    b = c // 4
    x = np.asarray(x_full[b], dtype=np.float32)
    blk = lambda i: x[512 * i:512 * (i + 1)]
    own = np.concatenate([blk(p), blk(7 - p)], axis=0)
    pos_own = np.concatenate([np.arange(512 * p, 512 * (p + 1)),
                              np.arange(512 * (7 - p), 512 * (8 - p))])
    storage = np.concatenate(
        [x[0:512 * (7 - p)], np.zeros((512 * p, D), np.float32),
         blk(7 - p), blk(p)], axis=0)
    pos_st = np.concatenate(
        [np.arange(0, 512 * (7 - p)), np.zeros(512 * p, np.int64),
         np.arange(512 * (7 - p), 512 * (8 - p)),
         np.arange(512 * p, 512 * (p + 1))])
    cosq = np.zeros((128, NT), np.float32)
    ct = _cos_table(pos_own).T
    cosq[64:80, :] = ct
    cosq[80:96, :] = ct
    cosk = _cos_table(pos_st) / H
    biasa = np.full((128, 16), -M0, np.float32)
    for it in range(12):
        if it >= 4 * p:
            biasa[:, it] = NEG
    biasb = np.full((128, 32), -M0, np.float32)
    for t in range(32):
        if 4 * (7 - p) <= t < 28:
            biasb[:, t] = NEG
    return {
        "own_tm": np.ascontiguousarray(own),
        "own_fm": np.ascontiguousarray(own.T),
        "xp_tm": np.ascontiguousarray(storage),
        "xp_fm": np.ascontiguousarray(storage.T),
        "cosq": cosq,
        "cosk": np.ascontiguousarray(cosk),
        "biasa": biasa,
        "biasb": biasb,
    }


_CACHED = {"nc": None}


def kernel(**inputs):
    if _CACHED["nc"] is None:
        _CACHED["nc"] = KB().build()
    nc = _CACHED["nc"]
    shared = _pack_shared(inputs)
    x_full = np.asarray(inputs["L0_attn_x"], dtype=np.float32)
    in_maps = []
    for c in range(8):
        m = dict(shared)
        m.update(_pack_core(c, x_full))
        in_maps.append(m)
    res = run_bass_kernel_spmd(nc, in_maps, core_ids=list(range(8)))
    out = np.empty((B, S, D), np.float32)
    for c in range(8):
        p, b = c % 4, c // 4
        o = res.results[c]["out"]
        out[b, 512 * p:512 * (p + 1)] = o[0:512]
        out[b, 512 * (7 - p):512 * (8 - p)] = o[512:1024]
    return out
